# version 11
# baseline (speedup 1.0000x reference)
import contextlib
import numpy as np
import ml_dtypes
import concourse.bass as bass
import concourse.mybir as mybir
from concourse.bass_utils import run_bass_kernel_spmd

F32 = mybir.dt.float32
BF16 = mybir.dt.bfloat16
AF = mybir.ActivationFunctionType
ALU = mybir.AluOpType
AX = mybir.AxisListType

D = 1024
S = 2048
SD = 256
FF = 4096
NE = 8
NK = D // 128
NT = 4
TW = 512
SP = S + 2
FCW = 2048
NMF = FCW // 128
EPS = 1e-5

ENGS = ("sync", "gpsimd", "tensor", "scalar", "vector")


class Sem:
    def __init__(self, h):
        self.h = h
        self.n = 0

    def inc(self, k=1):
        self.n += k
        return self.n


class Prog:
    def __init__(self):
        self.q = {e: [] for e in ENGS}
        self.lastw = {}

    def do(self, eng, fn):
        self.q[eng].append(fn)

    def wait(self, eng, sem, v):
        if v <= 0:
            return
        key = (eng, id(sem))
        if self.lastw.get(key, -1) >= v:
            return
        self.lastw[key] = v
        self.q[eng].append(lambda E, h=sem.h, v=v: E.wait_ge(h, v))


class Pool:

    def __init__(self, prog, slots):
        self.prog = prog
        self.slots = [{"ap": s, "free": None} for s in slots]
        self.i = 0

    def acquire(self):
        s = self.slots[self.i % len(self.slots)]
        self.i += 1
        if s["free"] is not None:
            self.prog.wait("tensor", s["free"][0], s["free"][1])
        return s


def build(taps=False):
    nc = bass.Bass("TRN2", target_bir_lowering=False)

    dX = nc.dram_tensor("x", [S, D], F32, kind="ExternalInput").ap()
    dLNG = nc.dram_tensor("ln_g", [1, D], F32, kind="ExternalInput").ap()
    dLNB = nc.dram_tensor("ln_b", [1, D], F32, kind="ExternalInput").ap()
    dMIXW = nc.dram_tensor("mix_w", [D, 3], F32, kind="ExternalInput").ap()
    dMIXB = nc.dram_tensor("mix_b", [1, D], F32, kind="ExternalInput").ap()
    dWI = nc.dram_tensor("Wi", [D, D], F32, kind="ExternalInput").ap()
    dBI = nc.dram_tensor("bi", [1, D], F32, kind="ExternalInput").ap()
    dWSIN = nc.dram_tensor("Wsin", [D, SD], F32, kind="ExternalInput").ap()
    dBSIN = nc.dram_tensor("bsin", [1, SD], F32, kind="ExternalInput").ap()
    dWD = nc.dram_tensor("Wd", [D, SD], F32, kind="ExternalInput").ap()
    dBD = nc.dram_tensor("bd", [1, SD], F32, kind="ExternalInput").ap()
    dWSEL = nc.dram_tensor("Wsel", [D, SD], F32, kind="ExternalInput").ap()
    dBSEL = nc.dram_tensor("bsel", [1, SD], F32, kind="ExternalInput").ap()
    dWSO = nc.dram_tensor("Wso", [SD, D], F32, kind="ExternalInput").ap()
    dBSO = nc.dram_tensor("bso", [1, D], F32, kind="ExternalInput").ap()
    dWG = nc.dram_tensor("Wg", [D, D], F32, kind="ExternalInput").ap()
    dBG = nc.dram_tensor("bg", [1, D], F32, kind="ExternalInput").ap()
    dWOUT = nc.dram_tensor("Wout", [D, D], F32, kind="ExternalInput").ap()
    dBOUT = nc.dram_tensor("bout", [1, D], F32, kind="ExternalInput").ap()
    dWRED = nc.dram_tensor("w_red", [D, 16], F32, kind="ExternalInput").ap()
    dWG8 = nc.dram_tensor("wg8", [8, 16], F32, kind="ExternalInput").ap()
    dW1 = nc.dram_tensor("w1", [NE, D, FF], F32, kind="ExternalInput").ap()
    dB1 = nc.dram_tensor("b1", [NE, FF], F32, kind="ExternalInput").ap()
    dW2 = nc.dram_tensor("w2", [NE, FF, D], F32, kind="ExternalInput").ap()
    dB2 = nc.dram_tensor("b2", [NE, D], F32, kind="ExternalInput").ap()
    dIDENT = nc.dram_tensor("ident", [128, 128], F32, kind="ExternalInput").ap()
    dIDENTB = nc.dram_tensor("identb", [8, 8], BF16, kind="ExternalInput").ap()
    dE8 = nc.dram_tensor("e8", [8, 1024], BF16, kind="ExternalInput").ap()
    dONESF = nc.dram_tensor("ones_f", [1, 128], F32, kind="ExternalInput").ap()
    dONESB = nc.dram_tensor("ones_b", [1, TW], BF16, kind="ExternalInput").ap()
    dONESC = nc.dram_tensor("ones_c", [128, 1], BF16, kind="ExternalInput").ap()
    dZEROC = nc.dram_tensor("zero_c", [128, 1], F32, kind="ExternalInput").ap()

    dY = nc.dram_tensor("y", [S, D], F32, kind="ExternalOutput").ap()

    tap_t = {}
    if taps:
        for name, shape in [
            ("tA", [128, NK * S]), ("tH", [128, NK * SP]), ("tMIX", [128, NK * TW]),
            ("tDEC", [128, 2 * TW]), ("tST", [128, 2 * TW]), ("tU", [128, NK * TW]),
            ("tG", [128, NK * TW]), ("tPA", [128, NK * S]), ("tQ", [128, NK * S]),
            ("tMS", [8, S]), ("tYA", [128, NK * S]), ("tSB", [128, NK * S]),
            ("tRED", [16, S]), ("tLG", [8, S]), ("tSCR", [128, 16]),
        ]:
            tap_t[name] = nc.dram_tensor(name, shape, F32, kind="ExternalOutput").ap()

    P = Prog()
    es = contextlib.ExitStack()
    with es:
        def mksb(stack):
            return lambda name, shape, dt: stack.enter_context(nc.sbuf_tensor(name, shape, dt))

        sb = mksb(es)

        Pb = sb("Pb", [128, NK * S], BF16)
        ident = sb("ident_sb", [128, 128], F32)
        identb = sb("identb_sb", [8, 8], BF16)
        e8 = sb("e8_sb", [8, 1024], BF16)
        ones_f = sb("ones_f_sb", [1, 128], F32)
        ones_c = sb("ones_c_sb", [128, 1], BF16)
        zero_c = sb("zero_c_sb", [128, 1], F32)
        mu_r = sb("mu_r", [1, TW], F32)
        e2e_r = sb("e2e_r", [1, TW], F32)
        rstd_r = sb("rstd_r", [1, TW], F32)
        nrm_s = sb("nrm_s", [128, TW], F32)
        sq_bf = sb("sq_bf", [128, 2 * TW], BF16)
        lng_c = sb("lng_c", [128, NK], F32)
        lnb_c = sb("lnb_c", [128, NK], F32)
        mixw_c = sb("mixw_c", [128, NK * 3], F32)
        mixb_c = sb("mixb_c", [128, NK], F32)
        bi_c = sb("bi_c", [128, NK], F32)
        bg_c = sb("bg_c", [128, NK], F32)
        bout_c = sb("bout_c", [128, NK], F32)
        bso_c = sb("bso_c", [128, NK], F32)
        bsin_c = sb("bsin_c", [128, 2], F32)
        bd_c = sb("bd_c", [128, 2], F32)
        bsel_c = sb("bsel_c", [128, 2], F32)
        wg8_s = sb("wg8_s", [8, 16], F32)
        wred_b = sb("wred_b", [128, NK * 16], BF16)

        psA = es.enter_context(nc.psum_tensor("psA", [128, 1024], F32))
        psB = es.enter_context(nc.psum_tensor("psB", [128, 1024], F32))
        psC = es.enter_context(nc.psum_tensor("psC", [128, 1024], F32))
        psD = es.enter_context(nc.psum_tensor("psD", [128, 512], F32))
        psE = es.enter_context(nc.psum_tensor("psE", [128, 512], BF16))

        s_ld = Sem(es.enter_context(nc.semaphore("s_ld")))
        s_ls = Sem(es.enter_context(nc.semaphore("s_ls")))
        s_gl = Sem(es.enter_context(nc.semaphore("s_gl")))
        s_gp = Sem(es.enter_context(nc.semaphore("s_gp")))
        s_x = [Sem(es.enter_context(nc.semaphore(f"s_x{i}"))) for i in range(3)]
        s_w1 = [Sem(es.enter_context(nc.semaphore(f"s_w1{i}"))) for i in range(3)]
        s_w2 = [Sem(es.enter_context(nc.semaphore(f"s_w2{i}"))) for i in range(2)]
        s_b = Sem(es.enter_context(nc.semaphore("s_b")))
        s_o = [Sem(es.enter_context(nc.semaphore(f"s_o{i}"))) for i in range(2)]
        s_pe = Sem(es.enter_context(nc.semaphore("s_pe")))
        s_sc = Sem(es.enter_context(nc.semaphore("s_sc")))
        s_ve = Sem(es.enter_context(nc.semaphore("s_ve")))
        s_st = Sem(es.enter_context(nc.semaphore("s_st")))

        pool = Pool(P, [psA[:, 0:512], psA[:, 512:1024],
                        psB[:, 0:512], psB[:, 512:1024]])

        def sdma(dst, src):
            v = s_ld.inc(16)
            P.do("sync", lambda E, d=dst, s=src, h=s_ld.h: E.dma_start(d, s).then_inc(h, 16))
            return v

        def gdma(dst, src):
            v = s_gl.inc(16)
            P.do("gpsimd", lambda E, d=dst, s=src, h=s_gl.h: E.dma_start(d, s).then_inc(h, 16))
            return v

        def scdma(dst, src):
            v = s_ls.inc(16)
            P.do("scalar", lambda E, d=dst, s=src, h=s_ls.h: E.dma_start(d, s).then_inc(h, 16))
            return v

        def tap_dump(name, src_ap, wait_sem, wait_v):
            if not taps:
                return
            P.wait("gpsimd", wait_sem, wait_v)
            s_st.inc(16)
            P.do("gpsimd", lambda E, d=tap_t[name][:, :], s=src_ap, h=s_st.h:
                 E.dma_start(d, s).then_inc(h, 16))

        v_const = 0
        for dst, src in [(ident, dIDENT), (identb, dIDENTB), (e8, dE8), (ones_f, dONESF),
                         (ones_c, dONESC), (zero_c, dZEROC),
                         (wg8_s, dWG8)]:
            v_const = sdma(dst[:, :], src[:, :])
        v_mixw = sdma(mixw_c[:, :].rearrange("p (c j) -> p c j", c=NK),
                      dMIXW.rearrange("(c p) j -> p c j", p=128))
        v_wred = gdma(wred_b[:, :].rearrange("p (c m) -> p c m", c=NK),
                      dWRED.rearrange("(c p) m -> p c m", p=128))

        es_attn = es.enter_context(contextlib.ExitStack())
        sbA = mksb(es_attn)
        xstage = sbA("xstage", [128, 3 * 1024], F32)
        wi_b = sbA("wi_b", [128, NK * 1024], BF16)
        wg_b = sbA("wg_b", [128, NK * 1024], BF16)
        wout_b = sbA("wout_b", [128, NK * 1024], BF16)
        wsin_b = sbA("wsin_b", [128, NK * SD], BF16)
        wd_b = sbA("wd_b", [128, NK * SD], BF16)
        wsel_b = sbA("wsel_b", [128, NK * SD], BF16)
        wso_b = sbA("wso_b", [128, 2 * 1024], BF16)
        h_pad = sbA("h_pad", [128, NK * SP], BF16)
        mixed_bf = sbA("mixed_bf", [128, 2 * NK * TW], BF16)
        u_bf = sbA("u_bf", [128, NK * TW], BF16)
        g_bf = sbA("g_bf", [128, NK * TW], BF16)
        dec_f = sbA("dec_f", [128, 2 * TW], F32)
        xst_f = sbA("xst_f", [128, 2 * TW], F32)
        sel_f = sbA("sel_f", [128, 2 * TW], BF16)
        st_f = sbA("st_f", [128, 2 * TW], F32)
        ss_bf = sbA("ss_bf", [128, 2 * TW], BF16)
        carry = sbA("carry", [128, 2], F32)
        s0_bf = sbA("s0_bf", [128, TW], BF16)
        ctmp = sbA("ctmp", [128, TW], BF16)

        v_w = 0
        for dst, src, w in [(wsin_b, dWSIN, SD), (wd_b, dWD, SD), (wsel_b, dWSEL, SD)]:
            v_w = gdma(dst[:, :].rearrange("p (c m) -> p c m", c=NK),
                       src.rearrange("(c p) m -> p c m", p=128))
        v_w = gdma(wso_b[:, :].rearrange("p (c m) -> p c m", c=2),
                   dWSO.rearrange("(c p) m -> p c m", p=128))
        for dst, src in [(wi_b, dWI), (wg_b, dWG), (wout_b, dWOUT)]:
            v_w = gdma(dst[:, :].rearrange("p (c m) -> p c m", c=NK),
                       src.rearrange("(c p) m -> p c m", p=128))
        v_attn_w = v_w

        P.wait("tensor", s_ld, v_const)

        def row_to_cols(row_of, n, dst):
            def t_ops(E, srcs=[row_of(i) for i in range(n)]):
                for i, src in enumerate(srcs):
                    ins = E.transpose(psD[:, i:i + 1], src, ident[0:1, 0:1])
                return ins
            P.do("tensor", lambda E, f=t_ops: f(E).then_inc(s_pe.h, 1))
            pv = s_pe.inc()
            P.wait("scalar", s_pe, pv)
            P.do("scalar", lambda E, d=dst, n=n: E.copy(d, psD[:, 0:n]).then_inc(s_sc.h, 1))
            return s_sc.inc()

        row_waves = [
            [(dLNG, lng_c, NK), (dLNB, lnb_c, NK), (dMIXB, mixb_c, NK)],
            [(dBI, bi_c, NK), (dBG, bg_c, NK), (dBOUT, bout_c, NK)],
            [(dBSO, bso_c, NK), (dBSIN, bsin_c, 2), (dBD, bd_c, 2),
             (dBSEL, bsel_c, 2)],
        ]
        prev_cols = 0
        wave_pe = 0
        for wave in row_waves:
            P.wait("sync", s_pe, wave_pe)
            off = 0
            offs = []
            v_wave = 0
            for src, dst, n in wave:
                offs.append(off)
                v_wave = sdma(xstage[0:1, off:off + n * 128], src[0:1, :])
                off += n * 128
            P.wait("tensor", s_ld, v_wave)
            for (src, dst, n), o in zip(wave, offs):
                P.wait("tensor", s_sc, prev_cols)
                prev_cols = row_to_cols(
                    lambda j, o=o: xstage[0:1, o + j * 128:o + (j + 1) * 128],
                    n, dst[:, 0:n])
            wave_pe = s_pe.n
        v_cols = prev_cols
        v_bias_cols = prev_cols

        sqslot_free = {}
        ln_state = {"prev": 0}

        def emit_ln(src_ap, dst_ap, gain, out_chunks=NK, tiles=None):
            if tiles is None:
                tiles = range(NT)
            ln_done = 0
            for t in tiles:
                P.wait("tensor", s_ve, ln_state["prev"])
                pv = 0
                for k in range(NK):
                    slot = k % 2
                    if slot in sqslot_free:
                        P.wait("scalar", s_pe, sqslot_free[slot])
                    P.do("scalar", lambda E, k=k, t=t, slot=slot: E.activation(
                        sq_bf[:, slot * TW:(slot + 1) * TW], src_ap(k, t),
                        AF.Square).then_inc(s_sc.h, 1))
                    sv = s_sc.inc()
                    P.do("tensor", lambda E, k=k, t=t: E.matmul(
                        psC[0:1, 0:TW], ones_c[:, :], src_ap(k, t),
                        start=(k == 0), stop=(k == NK - 1)))
                    P.wait("tensor", s_sc, sv)
                    P.do("tensor", lambda E, k=k, slot=slot: E.matmul(
                        psC[0:1, 512:512 + TW], ones_c[:, :],
                        sq_bf[:, slot * TW:(slot + 1) * TW],
                        start=(k == 0), stop=(k == NK - 1)).then_inc(s_pe.h, 1))
                    pv = s_pe.inc()
                    sqslot_free[slot] = pv
                v_stats = pv
                P.wait("vector", s_pe, v_stats)
                P.do("vector", lambda E: E.tensor_scalar(mu_r[:, :], psC[0:1, 0:TW],
                                                         1.0 / D, None, ALU.mult))
                P.do("vector", lambda E: E.tensor_scalar(e2e_r[:, :], psC[0:1, 512:512 + TW],
                                                         1.0 / D, EPS, ALU.mult, ALU.add))
                P.do("vector", lambda E: E.tensor_tensor(rstd_r[:, :], mu_r[:, :], mu_r[:, :],
                                                         ALU.mult))
                P.do("vector", lambda E: E.tensor_tensor(e2e_r[:, :], e2e_r[:, :], rstd_r[:, :],
                                                         ALU.subtract).then_inc(s_ve.h, 1))
                vv = s_ve.inc()
                P.wait("scalar", s_ve, vv)
                P.do("scalar", lambda E: E.activation(rstd_r[:, :], e2e_r[:, :],
                                                      AF.Sqrt).then_inc(s_sc.h, 1))
                sv2 = s_sc.inc()
                P.wait("vector", s_sc, sv2)
                P.do("vector", lambda E: E.reciprocal(e2e_r[:, :], rstd_r[:, :]))
                P.do("vector", lambda E: E.tensor_tensor(mu_r[:, :], mu_r[:, :], e2e_r[:, :],
                                                         ALU.mult).then_inc(s_ve.h, 1))
                vv2 = s_ve.inc()
                P.wait("tensor", s_ve, vv2)
                P.do("tensor", lambda E: E.matmul(psC[:, 0:TW], ones_f[0:1, 0:128],
                                                  e2e_r[:, :], start=True, stop=True))
                P.do("tensor", lambda E: E.matmul(psC[:, 512:512 + TW], ones_f[0:1, 0:128],
                                                  mu_r[:, :], start=True,
                                                  stop=True).then_inc(s_pe.h, 1))
                v_bc = s_pe.inc()
                P.wait("vector", s_pe, v_bc)
                for k in range(out_chunks):
                    P.do("vector", lambda E, k=k, t=t: E.tensor_tensor(
                        nrm_s[:, :], src_ap(k, t), psC[:, 0:TW], ALU.mult))
                    if gain:
                        P.do("vector", lambda E, k=k, t=t: E.tensor_tensor(
                            nrm_s[:, :], nrm_s[:, :], psC[:, 512:512 + TW], ALU.subtract))
                        P.do("vector", lambda E, k=k, t=t: E.tensor_scalar(
                            dst_ap(k, t), nrm_s[:, :], lng_c[:, k:k + 1], lnb_c[:, k:k + 1],
                            ALU.mult, ALU.add))
                    else:
                        P.do("vector", lambda E, k=k, t=t: E.tensor_tensor(
                            dst_ap(k, t), nrm_s[:, :], psC[:, 512:512 + TW], ALU.subtract))
                P.do("vector", lambda E: E.engine_nop().then_inc(s_ve.h, 1))
                ln_done = s_ve.inc()
                ln_state["prev"] = ln_done
            return ln_done

        A_src = lambda k, t: Pb[:, k * S + t * TW: k * S + (t + 1) * TW]
        h_dst = lambda k, t: h_pad[:, k * SP + 1 + t * TW: k * SP + 1 + (t + 1) * TW]

        for k in range(NK):
            P.do("vector", lambda E, k=k: E.memset(h_pad[:, k * SP:k * SP + 1], 0.0))
            P.do("vector", lambda E, k=k: E.memset(
                h_pad[:, k * SP + 1 + S:k * SP + 2 + S], 0.0))
        P.do("vector", lambda E: E.engine_nop().then_inc(s_ve.h, 1))
        s_ve.inc()
        P.do("scalar", lambda E: E.memzero(carry[:, :]).then_inc(s_sc.h, 1))
        v_carry = s_sc.inc()

        MW = NK * TW
        mix_rhs = lambda k, par: mixed_bf[:, par * MW + k * TW: par * MW + (k + 1) * TW]
        cpe_hist = [0, 0]
        conv_done = {}

        def emit_conv(t):
            par = t % 2
            if t >= 2:
                P.wait("vector", s_pe, cpe_hist[par])
            for k in range(NK):
                base = k * SP + t * TW
                P.do("vector", lambda E, k=k, base=base: E.tensor_scalar(
                    s0_bf[:, 0:TW], h_pad[:, base: base + TW],
                    mixw_c[:, k * 3:k * 3 + 1], mixb_c[:, k:k + 1],
                    ALU.mult, ALU.add))
                P.do("vector", lambda E, k=k, base=base: E.scalar_tensor_tensor(
                    ctmp[:, 0:TW], h_pad[:, base + 1: base + 1 + TW],
                    mixw_c[:, k * 3 + 1:k * 3 + 2], s0_bf[:, 0:TW],
                    ALU.mult, ALU.add))
                def g3(E, k=k, base=base, par=par):
                    return E.scalar_tensor_tensor(
                        mix_rhs(k, par), h_pad[:, base + 2: base + 2 + TW],
                        mixw_c[:, k * 3 + 2:k * 3 + 3], ctmp[:, 0:TW],
                        ALU.mult, ALU.add)
                P.do("vector", lambda E, f=g3: f(E).then_inc(s_ve.h, 1))
                s_ve.inc()
            conv_done[t] = s_ve.n

        P.wait("sync", s_pe, wave_pe)
        P.wait("scalar", s_pe, wave_pe)
        xs_free = [None, None, None]
        ln1_marks = {}
        for g in range(16):
            slot = g % 3
            q = "sync" if g % 2 == 0 else "scalar"
            if xs_free[slot] is not None:
                P.wait(q, xs_free[slot][0], xs_free[slot][1])
            dstx = xstage[:, slot * 1024:(slot + 1) * 1024]
            srcx = dX[g * 128:(g + 1) * 128, :]
            vx = s_x[slot].inc(16)
            P.do(q, lambda E, d=dstx, s=srcx, h=s_x[slot].h:
                 E.dma_start(d, s).then_inc(h, 16))
            P.wait("tensor", s_x[slot], vx)
            for half in range(2):
                sl = pool.acquire()
                def t_tr(E, sl=sl, slot=slot, half=half):
                    for j in range(4):
                        k = half * 4 + j
                        ins = E.transpose(sl["ap"][:, j * 128:(j + 1) * 128],
                                          xstage[:, slot * 1024 + k * 128:
                                                 slot * 1024 + (k + 1) * 128],
                                          ident[:, :])
                    return ins
                P.do("tensor", lambda E, f=t_tr: f(E).then_inc(s_pe.h, 1))
                pv = s_pe.inc()
                P.wait("scalar", s_pe, pv)
                out_ap = Pb[:, :].rearrange("p (c s) -> p c s", c=NK)[
                    :, half * 4:(half + 1) * 4, g * 128:(g + 1) * 128]
                P.do("scalar", lambda E, o=out_ap, sl=sl: E.copy(
                    o, sl["ap"][:, 0:512].rearrange("p (c s) -> p c s", c=4)).then_inc(s_sc.h, 1))
                sv = s_sc.inc()
                sl["free"] = (s_sc, sv)
                if half == 1:
                    xs_free[slot] = (s_sc, sv)
            if g % 4 == 3:
                t_ln = g // 4
                xmark = s_sc.n
                P.wait("tensor", s_sc, xmark)
                P.wait("vector", s_sc, xmark)
                ln1_marks[t_ln] = emit_ln(A_src, h_dst, gain=True, tiles=[t_ln])
                if t_ln in (1, 2):
                    emit_conv(t_ln - 1)
        v_A_done = s_sc.n
        v_ln1 = ln1_marks[NT - 1]
        tap_dump("tA", Pb[:, :], s_sc, v_A_done)
        tap_dump("tH", h_pad[:, :], s_ve, v_ln1)

        P.wait("tensor", s_gl, v_attn_w)
        P.wait("scalar", s_ve, v_ln1)
        prev = {"dxs_ve": 0, "sel_ve": 0, "g_ve": 0}
        ss_rhs = lambda k: ss_bf[:, k * TW:(k + 1) * TW]

        def emit_proj(wbuf, wwidth, bcol, n_k, m, act, dst_ap, v_rhs, par):
            sl = pool.acquire()
            P.wait("tensor", s_ve, v_rhs)
            def mmf(E, sl=sl, wbuf=wbuf, wwidth=wwidth, n_k=n_k, m=m, par=par):
                for k in range(n_k):
                    ins = E.matmul(sl["ap"],
                                   wbuf[:, k * wwidth + m * 128: k * wwidth + (m + 1) * 128],
                                   mix_rhs(k, par), start=(k == 0), stop=(k == n_k - 1))
                return ins
            P.do("tensor", lambda E, f=mmf: f(E).then_inc(s_pe.h, 1))
            pv = s_pe.inc()
            P.wait("scalar", s_pe, pv)
            P.do("scalar", lambda E, sl=sl, act=act, dst=dst_ap, bcol=bcol: E.activation(
                dst, sl["ap"], act, bias=bcol, scale=1.0).then_inc(s_sc.h, 1))
            sv = s_sc.inc()
            sl["free"] = (s_sc, sv)
            return sv

        for t in range(NT):
            par = t % 2
            v_mix = conv_done[t]

            P.wait("scalar", s_ve, prev["dxs_ve"])
            v_xst = [emit_proj(wsin_b, SD, bsin_c[:, m:m + 1], NK, m, AF.Tanh,
                               xst_f[:, m * TW:(m + 1) * TW], v_mix, par) for m in range(2)]
            v_dec = [emit_proj(wd_b, SD, bd_c[:, m:m + 1], NK, m, AF.Sigmoid,
                               dec_f[:, m * TW:(m + 1) * TW], v_mix, par) for m in range(2)]
            P.wait("vector", s_sc, v_dec[1])
            P.wait("vector", s_sc, v_carry)
            for c in range(2):
                P.do("vector", lambda E, c=c: E.tensor_tensor(
                    nrm_s[:, :], dec_f[:, c * TW:(c + 1) * TW],
                    xst_f[:, c * TW:(c + 1) * TW], ALU.mult))
                P.do("vector", lambda E, c=c: E.tensor_tensor(
                    xst_f[:, c * TW:(c + 1) * TW], xst_f[:, c * TW:(c + 1) * TW],
                    nrm_s[:, :], ALU.subtract))
                P.do("vector", lambda E, c=c: E.tensor_tensor_scan(
                    st_f[:, c * TW:(c + 1) * TW], dec_f[:, c * TW:(c + 1) * TW],
                    xst_f[:, c * TW:(c + 1) * TW], carry[:, c:c + 1], ALU.mult, ALU.add))
            P.do("vector", lambda E: E.engine_nop().then_inc(s_ve.h, 1))
            v_scan = s_ve.inc()
            prev["dxs_ve"] = v_scan
            P.wait("scalar", s_ve, v_scan)
            P.do("scalar", lambda E: E.copy(carry[:, 0:1], st_f[:, TW - 1: TW]))
            P.do("scalar", lambda E: E.copy(carry[:, 1:2],
                                            st_f[:, 2 * TW - 1: 2 * TW]).then_inc(s_sc.h, 1))
            v_carry = s_sc.inc()

            P.wait("scalar", s_ve, prev["g_ve"])
            v_u = [emit_proj(wi_b, 1024, bi_c[:, m:m + 1], NK, m, AF.Tanh,
                             u_bf[:, m * TW:(m + 1) * TW], v_mix, par) for m in range(NK)]
            P.wait("scalar", s_ve, prev["sel_ve"])
            v_sel = [emit_proj(wsel_b, SD, bsel_c[:, m:m + 1], NK, m, AF.Sigmoid,
                               sel_f[:, m * TW:(m + 1) * TW], v_mix, par) for m in range(2)]
            v_g = [emit_proj(wg_b, 1024, bg_c[:, m:m + 1], NK, m, AF.Sigmoid,
                             g_bf[:, m * TW:(m + 1) * TW], v_mix, par) for m in range(NK)]
            v_mix_consumed = s_pe.n

            if t == 0:
                tap_dump("tMIX", mixed_bf[:, 0:MW], s_pe, v_mix_consumed)
                tap_dump("tDEC", dec_f[:, :], s_ve, v_scan)
                tap_dump("tST", st_f[:, :], s_ve, v_scan)

            P.wait("vector", s_sc, v_g[NK - 1])
            for k in range(NK):
                P.do("vector", lambda E, k=k: E.tensor_tensor(
                    ctmp[:, 0:TW], u_bf[:, k * TW:(k + 1) * TW],
                    g_bf[:, k * TW:(k + 1) * TW], ALU.mult))
                P.do("vector", lambda E, k=k: E.tensor_tensor(
                    u_bf[:, k * TW:(k + 1) * TW], u_bf[:, k * TW:(k + 1) * TW],
                    ctmp[:, 0:TW], ALU.subtract))
            P.do("vector", lambda E: E.tensor_tensor(ss_bf[:, :], sel_f[:, :], st_f[:, :],
                                                     ALU.mult))
            P.do("vector", lambda E: E.engine_nop().then_inc(s_ve.h, 1))
            v_ss = s_ve.inc()
            prev["sel_ve"] = v_ss

            if t == 0:
                tap_dump("tU", u_bf[:, :], s_ve, v_ss)
                tap_dump("tG", g_bf[:, :], s_sc, v_g[NK - 1])

            v_c_ve = 0
            for m in range(NK):
                sl = pool.acquire()
                P.wait("tensor", s_ve, v_ss)
                def mmso(E, sl=sl, m=m):
                    for kc in range(2):
                        ins = E.matmul(sl["ap"],
                                       wso_b[:, kc * 1024 + m * 128: kc * 1024 + (m + 1) * 128],
                                       ss_rhs(kc), start=(kc == 0), stop=(kc == 1))
                    return ins
                P.do("tensor", lambda E, f=mmso: f(E).then_inc(s_pe.h, 1))
                pv = s_pe.inc()
                P.wait("vector", s_pe, max(pv, v_mix_consumed))
                P.do("vector", lambda E, sl=sl, m=m: E.scalar_tensor_tensor(
                    ctmp[:, 0:TW], sl["ap"],
                    bso_c[:, m:m + 1], g_bf[:, m * TW:(m + 1) * TW],
                    ALU.add, ALU.mult))
                P.do("vector", lambda E, sl=sl, m=m, par=par: E.tensor_tensor(
                    mix_rhs(m, par), ctmp[:, 0:TW],
                    u_bf[:, m * TW:(m + 1) * TW], ALU.add).then_inc(s_ve.h, 1))
                v_c_ve = s_ve.inc()
                sl["free"] = (s_ve, v_c_ve)
            prev["g_ve"] = v_c_ve

            for m in range(NK):
                sl = pool.acquire()
                P.wait("tensor", s_ve, v_c_ve)
                def mmout(E, sl=sl, m=m, par=par):
                    for k in range(NK):
                        ins = E.matmul(sl["ap"],
                                       wout_b[:, k * 1024 + m * 128: k * 1024 + (m + 1) * 128],
                                       mix_rhs(k, par), start=(k == 0), stop=(k == NK - 1))
                    return ins
                P.do("tensor", lambda E, f=mmout: f(E).then_inc(s_pe.h, 1))
                pv = s_pe.inc()
                P.wait("vector", s_pe, pv)
                P.do("vector", lambda E, sl=sl, m=m, t=t: E.scalar_tensor_tensor(
                    A_src(m, t), A_src(m, t), 2.0, sl["ap"], ALU.mult, ALU.add))
                P.do("vector", lambda E, m=m, t=t: E.tensor_scalar(
                    A_src(m, t), A_src(m, t), bout_c[:, m:m + 1], None,
                    ALU.add).then_inc(s_ve.h, 1))
                v_res = s_ve.inc()
                sl["free"] = (s_ve, v_res)
            cpe_hist[par] = s_pe.n
            if t + 2 < NT:
                emit_conv(t + 2)
        v_attn_done = s_ve.n
        v_attn_pe = s_pe.n
        tap_dump("tPA", Pb[:, :], s_ve, v_attn_done)

        es_attn.close()
        es_post = es.enter_context(contextlib.ExitStack())
        sbP = mksb(es_post)
        Qb = sbP("Qb", [128, NK * S], BF16)
        hfc = sbP("hfc", [128, NMF * S], BF16)
        ms_all = sbP("ms_all", [8, S], BF16)
        w1g = sbP("w1g", [128, 3 * NK * 256], BF16)
        w2g = sbP("w2g", [128, 2 * 16 * 256], BF16)
        ms_bc = sbP("ms_bc", [128, 2 * S], BF16)
        tmp_bf = sbP("tmp_bf", [128, TW], BF16)
        b1r_f = sbP("b1r_f", [1, FCW], BF16)
        b1c = sbP("b1c", [128, 64], F32)
        b2c = sbP("b2c", [128, 16], F32)

        Q_dst = lambda k, t: Qb[:, k * S + t * TW: k * S + (t + 1) * TW]
        P.wait("tensor", s_ve, v_attn_done)
        P.wait("scalar", s_ve, v_attn_done)
        v_ln2 = emit_ln(A_src, Q_dst, gain=False)
        tap_dump("tQ", Qb[:, :], s_ve, v_ln2)

        P.wait("gpsimd", s_ve, v_ln2)
        P.wait("gpsimd", s_pe, v_attn_pe)
        P.wait("gpsimd", s_sc, s_sc.n)
        w1_free = [None, None, None]
        w2_free = [None, None]

        def bdma(dst, srcap):
            v = s_b.inc(16)
            P.do("gpsimd", lambda E, d=dst, s=srcap, h=s_b.h:
                 E.dma_start(d, s).then_inc(h, 16))
            return v
        v_b1h0 = {0: bdma(b1r_f[:, :], dB1[0:1, 0:FCW])}
        v_b1h1 = {}
        v_b2r = {}

        msbc_half_prev = {0: 0, 1: 0}
        rt = {}
        msbc_done = {}
        bprep_done = {}
        bprep_prev = 0
        v_y_end = {}

        def emit_msbc(e, ts):
            par = e % 2
            if e >= 2:
                P.wait("scalar", s_ve, v_y_end[e - 2])
            for t in ts:
                half = t % 2
                P.wait("tensor", s_sc, msbc_half_prev[half])
                P.do("tensor", lambda E, e=e, t=t, half=half: E.matmul(
                    psC[:, half * 512: half * 512 + TW], e8[0:8, e * 128:(e + 1) * 128],
                    ms_all[:, t * TW:(t + 1) * TW], start=True,
                    stop=True).then_inc(s_pe.h, 1))
                pv = s_pe.inc()
                P.wait("scalar", s_pe, pv)
                P.do("scalar", lambda E, par=par, t=t, half=half: E.copy(
                    ms_bc[:, par * S + t * TW: par * S + (t + 1) * TW],
                    psC[:, half * 512: half * 512 + TW]).then_inc(s_sc.h, 1))
                msbc_half_prev[half] = s_sc.inc()
            if ts[-1] == NT - 1:
                msbc_done[e] = s_sc.n

        def emit_bprep(e):
            par = e % 2
            global_b = bprep_prev
            P.wait("tensor", s_b, v_b1h0[e])
            P.wait("tensor", s_sc, global_b)
            if e >= 2:
                P.wait("scalar", s_ve, v_y_end[e - 2])
            P.wait("tensor", s_sc, rt.get("lgtm", 0))
            def mmtr_b1(E, off):
                for i in range(16):
                    c = 2 * (off + i)
                    ins = E.transpose(psE[:, c:c + 1],
                                      b1r_f[0:1, i * 128:(i + 1) * 128],
                                      identb[0:1, 0:1])
                return ins
            P.do("tensor", lambda E: mmtr_b1(E, 0).then_inc(s_pe.h, 1))
            pv1 = s_pe.inc()
            P.wait("gpsimd", s_pe, pv1)
            v_b1h1[e] = bdma(b1r_f[:, :], dB1[e:e + 1, FCW:FF])
            P.wait("tensor", s_b, v_b1h1[e])
            P.do("tensor", lambda E: mmtr_b1(E, 16).then_inc(s_pe.h, 1))
            pv2 = s_pe.inc()
            P.wait("gpsimd", s_pe, pv2)
            v_b2r[e] = bdma(b1r_f[0:1, 0:D], dB2[e:e + 1, :])
            P.wait("tensor", s_b, v_b2r[e])
            def mmtr_b2(E):
                for i in range(8):
                    ins = E.transpose(psE[:, 64 + 2 * i:65 + 2 * i],
                                      b1r_f[0:1, i * 128:(i + 1) * 128],
                                      identb[0:1, 0:1])
                return ins
            P.do("tensor", lambda E, f=mmtr_b2: f(E).then_inc(s_pe.h, 1))
            pv_b = s_pe.inc()
            P.wait("scalar", s_pe, pv_b)
            ev2 = psE[:, :].rearrange("p (i two) -> p i two", two=2)[:, :, 0]
            P.do("scalar", lambda E, par=par, ev2=ev2: E.copy(
                b1c[:, par * 32:(par + 1) * 32], ev2[:, 0:32]))
            P.do("scalar", lambda E, par=par, ev2=ev2: E.copy(
                b2c[:, par * 8:(par + 1) * 8], ev2[:, 32:40]).then_inc(s_sc.h, 1))
            bprep_done[e] = s_sc.inc()
            if e + 1 < NE:
                P.wait("gpsimd", s_pe, pv_b)
                v_b1h0[e + 1] = bdma(b1r_f[:, :], dB1[e + 1:e + 2, 0:FCW])
            return bprep_done[e]


        es_rt = es.enter_context(contextlib.ExitStack())
        sbR = mksb(es_rt)
        red_f = sbR("red_f", [16, S], BF16)
        lg_f = sbR("lg_f", [8, S], BF16)
        lgtm = sbR("lgtm", [128, 128], F32)
        mstm = sbR("mstm", [128, 128], F32)
        mx_c = sbR("mx_c", [128, 16], F32)
        ssum = sbR("ssum", [128, 16], F32)
        scrs = sbR("scrs", [128, 16], F32)
        exp_s = sbR("exp_s", [128, 8], F32)
        lgs_s = sbR("lgs_s", [128, 16], F32)
        wgT_f = sbR("wgT_f", [16, 8], BF16)
        nrm8 = sbR("nrm8", [8, 2], F32)
        rn8 = sbR("rn8", [8, 1], F32)
        wgsq = sbR("wgsq", [8, 16], F32)

        P.wait("tensor", s_gl, v_wred)
        P.wait("tensor", s_ve, v_ln2)
        red_ev = {}
        for t in range(NT):
            half = t % 2
            if t >= 2:
                P.wait("tensor", s_sc, red_ev[t - 2])
            def mmred(E, t=t, half=half):
                for k in range(NK):
                    ins = E.matmul(psC[0:16, half * 512: half * 512 + TW],
                                   wred_b[:, k * 16:(k + 1) * 16],
                                   Qb[:, k * S + t * TW: k * S + (t + 1) * TW],
                                   start=(k == 0), stop=(k == NK - 1))
                return ins
            P.do("tensor", lambda E, f=mmred: f(E).then_inc(s_pe.h, 1))
            pv = s_pe.inc()
            P.wait("scalar", s_pe, pv)
            P.do("scalar", lambda E, t=t, half=half: E.copy(
                red_f[:, t * TW:(t + 1) * TW],
                psC[0:16, half * 512: half * 512 + TW]).then_inc(s_sc.h, 1))
            red_ev[t] = s_sc.inc()
        v_red = red_ev[NT - 1]
        tap_dump("tRED", red_f[:, :], s_sc, v_red)

        P.wait("scalar", s_ld, v_const)
        P.do("scalar", lambda E: E.activation(wgsq[:, :], wg8_s[:, :],
                                              AF.Square).then_inc(s_sc.h, 1))
        v1 = s_sc.inc()
        P.wait("vector", s_sc, v1)
        P.do("vector", lambda E: E.reduce_sum(nrm8[:, 0:1], wgsq[:, :],
                                              axis=AX.X).then_inc(s_ve.h, 1))
        v2 = s_ve.inc()
        P.wait("scalar", s_ve, v2)
        P.do("scalar", lambda E: E.activation(nrm8[:, 1:2], nrm8[:, 0:1],
                                              AF.Sqrt).then_inc(s_sc.h, 1))
        v3 = s_sc.inc()
        P.wait("vector", s_sc, v3)
        P.do("vector", lambda E: E.tensor_single_scalar(nrm8[:, 1:2], nrm8[:, 1:2], 1e-4,
                                                        ALU.max).then_inc(s_ve.h, 1))
        v4 = s_ve.inc()
        P.wait("scalar", s_ve, v4)
        P.do("scalar", lambda E: E.activation(rn8[:, :], nrm8[:, 1:2],
                                              AF.Ln).then_inc(s_sc.h, 1))
        v4b = s_sc.inc()
        P.wait("vector", s_sc, v4b)
        P.do("vector", lambda E: E.tensor_scalar(rn8[:, :], rn8[:, :], -1.0, None,
                                                 ALU.mult).then_inc(s_ve.h, 1))
        v4c = s_ve.inc()
        P.wait("scalar", s_ve, v4c)
        P.do("scalar", lambda E: E.activation(rn8[:, :], rn8[:, :],
                                              AF.Exp).then_inc(s_sc.h, 1))
        v_rn8 = s_sc.inc()
        P.do("tensor", lambda E: E.transpose(psD[0:16, 0:8], wg8_s[:, :],
                                             ident[0:8, 0:8]).then_inc(s_pe.h, 1))
        v_wgT = s_pe.inc()
        P.wait("scalar", s_pe, v_wgT)
        P.do("scalar", lambda E: E.copy(wgT_f[:, :], psD[0:16, 0:8]).then_inc(s_sc.h, 1))
        v_wgTc = s_sc.inc()

        bprep_prev = v_wgTc
        bprep_prev = emit_bprep(0)

        P.wait("tensor", s_sc, bprep_done[0])
        P.wait("tensor", s_sc, v_red)
        P.wait("scalar", s_sc, v_rn8)
        lg_ev = {}
        for t in range(NT):
            if t >= 1:
                P.wait("tensor", s_ve, lg_ev[t - 1])
            P.do("tensor", lambda E, t=t: E.matmul(psD[0:8, 0:TW], wgT_f[:, :],
                                                   red_f[:, t * TW:(t + 1) * TW],
                                                   start=True, stop=True).then_inc(s_pe.h, 1))
            pv = s_pe.inc()
            P.wait("vector", s_pe, pv)
            P.wait("vector", s_sc, v_rn8)
            P.do("vector", lambda E, t=t: E.tensor_scalar(lg_f[:, t * TW:(t + 1) * TW],
                                                          psD[0:8, 0:TW], rn8[:, :], None,
                                                          ALU.mult).then_inc(s_ve.h, 1))
            lg_ev[t] = s_ve.inc()
        v_lg = lg_ev[NT - 1]
        tap_dump("tLG", lg_f[:, :], s_ve, v_lg)

        msbc_half_prev[0] = red_ev[2]
        msbc_half_prev[1] = red_ev[3]
        mx2 = sbR("mx2", [128, 16], F32)
        scr2 = sbR("scr2", [128, 16], F32)
        exp_done = {}
        ms_ev = {}

        def rt_lgT():
            P.wait("tensor", s_ve, v_lg)
            P.wait("tensor", s_sc, bprep_done[0])
            def mmtr_lg(E):
                for i in range(16):
                    ins = E.transpose(psE[:, i * 8:(i + 1) * 8],
                                      lg_f[0:8, i * 128:(i + 1) * 128], identb[0:8, 0:8])
                return ins
            P.do("tensor", lambda E: mmtr_lg(E).then_inc(s_pe.h, 1))
            v_lgT = s_pe.inc()
            P.wait("scalar", s_pe, v_lgT)
            P.do("scalar", lambda E: E.copy(lgtm[:, :], psE[:, 0:128]).then_inc(s_sc.h, 1))
            rt["lgtm"] = s_sc.inc()
            P.wait("vector", s_sc, rt["lgtm"])
            for i in range(16):
                P.do("vector", lambda E, i=i: E.reduce_max(
                    mx_c[:, i:i + 1], lgtm[:, i * 8:(i + 1) * 8], axis=AX.X))
            P.do("vector", lambda E: E.engine_nop().then_inc(s_ve.h, 1))
            v_mx = s_ve.inc()
            P.wait("scalar", s_ve, v_mx)
            P.do("scalar", lambda E: E.copy(mx2[:, :], mx_c[:, :]).then_inc(s_sc.h, 1))
            rt["mx2"] = s_sc.inc()

        def rt_exp(i0, i1):
            if i0 == 0:
                P.wait("vector", s_sc, rt["mx2"])
            for i in range(i0, i1):
                slot = i % 2
                if i >= 2:
                    P.wait("vector", s_sc, exp_done[i - 2])
                P.do("vector", lambda E, i=i, slot=slot: E.tensor_scalar(
                    lgs_s[:, slot * 8:(slot + 1) * 8], lgtm[:, i * 8:(i + 1) * 8],
                    mx2[:, i:i + 1], None, ALU.subtract).then_inc(s_ve.h, 1))
                vv = s_ve.inc()
                P.wait("scalar", s_ve, vv)
                P.do("scalar", lambda E, i=i, slot=slot: E.activation(
                    exp_s[:, :], lgs_s[:, slot * 8:(slot + 1) * 8], AF.Exp,
                    accum_out=ssum[:, i:i + 1]).then_inc(s_sc.h, 1))
                exp_done[i] = s_sc.inc()

        def rt_fin():
            v_esum = s_sc.n
            P.wait("vector", s_sc, v_esum)
            P.do("vector", lambda E: E.reciprocal(scrs[:, :], ssum[:, :]).then_inc(s_ve.h, 1))
            v_scr = s_ve.inc()
            P.wait("scalar", s_ve, v_scr)
            P.do("scalar", lambda E: E.copy(scr2[:, :], scrs[:, :]).then_inc(s_sc.h, 1))
            v_scr2 = s_sc.inc()
            P.wait("vector", s_sc, v_scr2)
            for i in range(16):
                P.do("vector", lambda E, i=i: E.tensor_scalar(
                    mstm[:, i * 8:(i + 1) * 8], lgtm[:, i * 8:(i + 1) * 8],
                    mx2[:, i:i + 1], None, ALU.is_equal))
                P.do("vector", lambda E, i=i: E.tensor_scalar(
                    mstm[:, i * 8:(i + 1) * 8], mstm[:, i * 8:(i + 1) * 8],
                    scr2[:, i:i + 1], None, ALU.mult))
            P.do("vector", lambda E: E.engine_nop().then_inc(s_ve.h, 1))
            rt["mstm"] = s_ve.inc()

        def rt_mstrans():
            P.wait("tensor", s_ve, rt["mstm"])
            for grp in range(4):
                if grp >= 1:
                    P.wait("tensor", s_sc, ms_ev[grp - 1])
                def mmtr_ms(E, grp=grp):
                    for j in range(4):
                        i = grp * 4 + j
                        ins = E.transpose(psD[0:8, j * 128:(j + 1) * 128],
                                          mstm[:, i * 8:(i + 1) * 8], ident[:, :])
                    return ins
                P.do("tensor", lambda E, f=mmtr_ms: f(E).then_inc(s_pe.h, 1))
                pv = s_pe.inc()
                P.wait("scalar", s_pe, pv)
                P.do("scalar", lambda E, grp=grp: E.copy(
                    ms_all[:, grp * 512:(grp + 1) * 512],
                    psD[0:8, 0:512]).then_inc(s_sc.h, 1))
                ms_ev[grp] = s_sc.inc()
            msbc_half_prev[0] = ms_ev[3]
            msbc_half_prev[1] = ms_ev[3]
            tap_dump("tMS", ms_all[:, :], s_sc, ms_ev[3])

        def rt_msbc():
            emit_msbc(0, [0, 1, 2, 3])

        rt_hooks = {1: rt_lgT, 2: lambda: rt_exp(0, 4), 3: lambda: rt_exp(4, 8),
                    4: lambda: rt_exp(8, 12),
                    5: lambda: (rt_exp(12, 16), rt_fin()),
                    6: rt_mstrans, 7: rt_msbc}

        es_rt.close()
        es_ex = es.enter_context(contextlib.ExitStack())
        sbE = mksb(es_ex)

        for e in range(NE):
            par_e = e % 2
            for fc in range(2):
                rt_hooks_here = rt_hooks if (e == 0 and fc == 0) else None
                def w1dma(gi, fc=fc, e=e):
                    slot = gi % 3
                    if w1_free[slot] is not None:
                        P.wait("gpsimd", s_pe, w1_free[slot])
                    src = dW1[e].rearrange("(k p) f -> p k f", p=128)[
                        :, :, fc * FCW + gi * 256: fc * FCW + (gi + 1) * 256]
                    dst = w1g[:, slot * NK * 256:(slot + 1) * NK * 256].rearrange(
                        "p (k j) -> p k j", k=NK)
                    v = s_w1[slot].inc(16)
                    P.do("gpsimd", lambda E, d=dst, s=src, h=s_w1[slot].h:
                         E.dma_start(d, s).then_inc(h, 16))
                    return v

                def w2dma(gi, fc=fc, e=e):
                    slot = gi % 2
                    if w2_free[slot] is not None:
                        P.wait("gpsimd", s_pe, w2_free[slot])
                    src = dW2[e].rearrange("(k p) f -> p k f", p=128)[
                        :, fc * 16:(fc + 1) * 16, gi * 256:(gi + 1) * 256]
                    dst = w2g[:, slot * 16 * 256:(slot + 1) * 16 * 256].rearrange(
                        "p (k j) -> p k j", k=16)
                    v = s_w2[slot].inc(16)
                    P.do("gpsimd", lambda E, d=dst, s=src, h=s_w2[slot].h:
                         E.dma_start(d, s).then_inc(h, 16))
                    return v

                w1_v = {gi: w1dma(gi) for gi in range(3)}
                w2_v = {}

                P.wait("scalar", s_sc, bprep_done[e])
                for gi in range(8):
                    slot = gi % 3
                    P.wait("tensor", s_w1[slot], w1_v[gi])
                    for mf2 in range(2):
                        mf = gi * 2 + mf2
                        gmf = fc * 16 + mf
                        for t in range(NT):
                            sl = pool.acquire()
                            def mmp1(E, sl=sl, slot=slot, mf2=mf2, t=t):
                                for k in range(NK):
                                    ins = E.matmul(
                                        sl["ap"],
                                        w1g[:, slot * NK * 256 + k * 256 + mf2 * 128:
                                            slot * NK * 256 + k * 256 + (mf2 + 1) * 128],
                                        Qb[:, k * S + t * TW: k * S + (t + 1) * TW],
                                        start=(k == 0), stop=(k == NK - 1))
                                return ins
                            P.do("tensor", lambda E, f=mmp1: f(E).then_inc(s_pe.h, 1))
                            pv = s_pe.inc()
                            P.wait("scalar", s_pe, pv)
                            P.do("scalar", lambda E, sl=sl, mf=mf, gmf=gmf, t=t,
                                 par_e=par_e: E.activation(
                                hfc[:, mf * S + t * TW: mf * S + (t + 1) * TW], sl["ap"],
                                AF.Gelu, bias=b1c[:, par_e * 32 + gmf: par_e * 32 + gmf + 1],
                                scale=1.0).then_inc(s_sc.h, 1))
                            sv = s_sc.inc()
                            sl["free"] = (s_sc, sv)
                    w1_free[slot] = s_pe.n
                    if gi + 3 < 8:
                        w1_v[gi + 3] = w1dma(gi + 3)
                    elif gi == 5:
                        w2_v[0] = w2dma(0)
                        w2_v[1] = w2dma(1)
                    if rt_hooks_here and gi in rt_hooks_here:
                        rt_hooks_here[gi]()
                v_h = s_sc.n
                if e + 1 < NE:
                    if fc == 0:
                        emit_msbc(e + 1, [0, 1])
                    else:
                        emit_msbc(e + 1, [2, 3])
                        bprep_prev = emit_bprep(e + 1)

                P.wait("vector", s_sc, max(msbc_done[e], bprep_done[e]))
                for gi in range(4):
                    slot = gi % 2
                    P.wait("tensor", s_w2[slot], w2_v[gi])
                    P.wait("tensor", s_sc, v_h)
                    for m2 in range(2):
                        m = gi * 2 + m2
                        for t in range(NT):
                            sl = pool.acquire()
                            def mmp2(E, sl=sl, slot=slot, m2=m2, t=t):
                                for kc in range(16):
                                    ins = E.matmul(
                                        sl["ap"],
                                        w2g[:, slot * 16 * 256 + kc * 256 + m2 * 128:
                                            slot * 16 * 256 + kc * 256 + (m2 + 1) * 128],
                                        hfc[:, kc * S + t * TW: kc * S + (t + 1) * TW],
                                        start=(kc == 0), stop=(kc == 15))
                                return ins
                            P.do("tensor", lambda E, f=mmp2: f(E).then_inc(s_pe.h, 1))
                            pv = s_pe.inc()
                            P.wait("vector", s_pe, pv)
                            bcol = b2c[:, par_e * 8 + m: par_e * 8 + m + 1] if fc == 0 else zero_c[:, :]
                            if e == 0 and fc == 0:
                                P.do("vector", lambda E, sl=sl, m=m, t=t, bcol=bcol,
                                     par_e=par_e: E.scalar_tensor_tensor(
                                         Pb[:, m * S + t * TW: m * S + (t + 1) * TW],
                                         sl["ap"], bcol,
                                         ms_bc[:, par_e * S + t * TW: par_e * S + (t + 1) * TW],
                                         ALU.add, ALU.mult).then_inc(s_ve.h, 1))
                            else:
                                P.do("vector", lambda E, sl=sl, t=t, bcol=bcol,
                                     par_e=par_e: E.scalar_tensor_tensor(
                                         tmp_bf[:, 0:TW], sl["ap"], bcol,
                                         ms_bc[:, par_e * S + t * TW: par_e * S + (t + 1) * TW],
                                         ALU.add, ALU.mult))
                                P.do("vector", lambda E, m=m, t=t: E.tensor_tensor(
                                    Pb[:, m * S + t * TW: m * S + (t + 1) * TW],
                                    Pb[:, m * S + t * TW: m * S + (t + 1) * TW],
                                    tmp_bf[:, 0:TW],
                                    ALU.add).then_inc(s_ve.h, 1))
                            y_ev = s_ve.inc()
                            sl["free"] = (s_ve, y_ev)
                    w2_free[slot] = s_pe.n
                    if gi + 2 < 4:
                        w2_v[gi + 2] = w2dma(gi + 2)
            v_y_end[e] = s_ve.n
        v_experts = s_ve.n
        tap_dump("tYA", Pb[:, :], s_ve, v_experts)

        s_dst = lambda k, t: hfc[:, k * S + t * TW: k * S + (t + 1) * TW]
        for k in range(NK):
            for t in range(NT):
                P.do("vector", lambda E, k=k, t=t: E.tensor_tensor(
                    s_dst(k, t), Q_dst(k, t), A_src(k, t), ALU.add))
        P.do("vector", lambda E: E.engine_nop().then_inc(s_ve.h, 1))
        v_sb = s_ve.inc()
        tap_dump("tSB", hfc[:, 0:NK * S], s_ve, v_sb)

        es_ex.close()
        es_fin = es.enter_context(contextlib.ExitStack())
        sbF = mksb(es_fin)
        of32 = sbF("of32", [128, NK * TW], F32)
        out_tm = sbF("out_tm", [128, 2 * 512], F32)
        ot_free = [None, None]

        P.wait("scalar", s_ve, v_sb)
        P.wait("tensor", s_ve, v_sb)
        P.wait("tensor", s_sc, s_sc.n)
        of_dst = lambda k, tt: of32[:, k * TW:(k + 1) * TW]
        for t in range(NT):
            v_ln3 = emit_ln(s_dst, of_dst, gain=False, tiles=[t])
            P.wait("tensor", s_ve, v_ln3)
            for gj in range(4):
                g = t * 4 + gj
                for half in range(2):
                    sl = pool.acquire()
                    def t_tr(E, sl=sl, gj=gj, half=half):
                        for j in range(4):
                            k = half * 4 + j
                            ins = E.transpose(sl["ap"][:, j * 128:(j + 1) * 128],
                                              of32[:, k * TW + gj * 128: k * TW + (gj + 1) * 128],
                                              ident[:, :])
                        return ins
                    P.do("tensor", lambda E, f=t_tr: f(E).then_inc(s_pe.h, 1))
                    pv = s_pe.inc()
                    P.wait("scalar", s_pe, pv)
                    if ot_free[half] is not None:
                        P.wait("scalar", ot_free[half][0], ot_free[half][1])
                    P.do("scalar", lambda E, sl=sl, half=half: E.copy(
                        out_tm[:, half * 512:(half + 1) * 512],
                        sl["ap"]).then_inc(s_sc.h, 1))
                    sv = s_sc.inc()
                    sl["free"] = (s_sc, sv)
                    P.wait("sync", s_sc, sv)
                    stv = s_o[half].inc(16)
                    P.do("sync", lambda E, g=g, half=half, h=s_o[half].h: E.dma_start(
                        dY[g * 128:(g + 1) * 128, half * 512:(half + 1) * 512],
                        out_tm[:, half * 512:(half + 1) * 512]).then_inc(h, 16))
                    ot_free[half] = (s_o[half], stv)
            P.wait("vector", s_pe, s_pe.n)

        P.wait("sync", s_o[0], s_o[0].n)
        P.wait("sync", s_o[1], s_o[1].n)

        with nc.Block() as blk:
            blk.sync(lambda E: [f(E) for f in P.q["sync"]])
            blk.gpsimd(lambda E: [f(E) for f in P.q["gpsimd"]])
            blk.tensor(lambda E: [f(E) for f in P.q["tensor"]])
            blk.scalar(lambda E: [f(E) for f in P.q["scalar"]])
            blk.vector(lambda E: [f(E) for f in P.q["vector"]])

    return nc


def kernel(**inputs):
    xs = np.ascontiguousarray(inputs["L0_attn_x"], dtype=np.float32)
    base = {
        "ln_g": np.ascontiguousarray(inputs["L0_attn_ln_g"], np.float32).reshape(1, D),
        "ln_b": np.ascontiguousarray(inputs["L0_attn_ln_b"], np.float32).reshape(1, D),
        "mix_w": np.ascontiguousarray(inputs["L0_attn_mix_w"], np.float32).reshape(D, 3),
        "mix_b": np.ascontiguousarray(inputs["L0_attn_mix_b"], np.float32).reshape(1, D),
        "Wi": np.ascontiguousarray(inputs["L0_attn_Wi"], np.float32),
        "bi": np.ascontiguousarray(inputs["L0_attn_bi"], np.float32).reshape(1, D),
        "Wsin": np.ascontiguousarray(inputs["L0_attn_Wsin"], np.float32),
        "bsin": np.ascontiguousarray(inputs["L0_attn_bsin"], np.float32).reshape(1, SD),
        "Wd": np.ascontiguousarray(inputs["L0_attn_Wd"], np.float32),
        "bd": np.ascontiguousarray(inputs["L0_attn_bd"], np.float32).reshape(1, SD),
        "Wsel": np.ascontiguousarray(inputs["L0_attn_Wsel"], np.float32),
        "bsel": np.ascontiguousarray(inputs["L0_attn_bsel"], np.float32).reshape(1, SD),
        "Wso": np.ascontiguousarray(inputs["L0_attn_Wso"], np.float32),
        "bso": np.ascontiguousarray(inputs["L0_attn_bso"], np.float32).reshape(1, D),
        "Wg": np.ascontiguousarray(inputs["L0_attn_Wg"], np.float32),
        "bg": np.ascontiguousarray(inputs["L0_attn_bg"], np.float32).reshape(1, D),
        "Wout": np.ascontiguousarray(inputs["L0_attn_Wout"], np.float32),
        "bout": np.ascontiguousarray(inputs["L0_attn_bout"], np.float32).reshape(1, D),
        "w_red": np.ascontiguousarray(inputs["L0_ffn_w_red"], np.float32),
        "wg8": np.ascontiguousarray(inputs["L0_ffn_wg"], np.float32),
        "w1": np.ascontiguousarray(inputs["L0_ffn_w1"], np.float32),
        "b1": np.ascontiguousarray(inputs["L0_ffn_b1"], np.float32),
        "w2": np.ascontiguousarray(inputs["L0_ffn_w2"], np.float32),
        "b2": np.ascontiguousarray(inputs["L0_ffn_b2"], np.float32),
        "ident": np.eye(128, dtype=np.float32),
        "identb": np.eye(8, dtype=np.float32).astype(ml_dtypes.bfloat16),
        "e8": np.repeat(np.eye(8, dtype=np.float32), 128, axis=1).astype(ml_dtypes.bfloat16),
        "ones_f": np.ones((1, 128), np.float32),
        "ones_b": np.ones((1, TW), ml_dtypes.bfloat16),
        "ones_c": np.ones((128, 1), ml_dtypes.bfloat16),
        "zero_c": np.zeros((128, 1), np.float32),
    }
    nc = build(taps=False)
    in_maps = [dict(base, x=np.ascontiguousarray(xs[b])) for b in range(8)]
    res = run_bass_kernel_spmd(nc, in_maps, core_ids=list(range(8)))
    return np.stack([np.asarray(r["y"], np.float32) for r in res.results], axis=0)


# version 12
# speedup vs baseline: 1.1259x; 1.1259x over previous
import contextlib
import numpy as np
import ml_dtypes
import concourse.bass as bass
import concourse.mybir as mybir
from concourse.bass_utils import run_bass_kernel_spmd

F32 = mybir.dt.float32
BF16 = mybir.dt.bfloat16
AF = mybir.ActivationFunctionType
ALU = mybir.AluOpType
AX = mybir.AxisListType

D = 1024
S = 2048
SD = 256
FF = 4096
NE = 8
NK = D // 128
NT = 4
TW = 512
SP = S + 2
FCW = 2048
NMF = FCW // 128
EPS = 1e-5

ENGS = ("sync", "gpsimd", "tensor", "scalar", "vector")


class Sem:
    def __init__(self, h):
        self.h = h
        self.n = 0

    def inc(self, k=1):
        self.n += k
        return self.n


class Prog:
    def __init__(self):
        self.q = {e: [] for e in ENGS}
        self.lastw = {}

    def do(self, eng, fn):
        self.q[eng].append(fn)

    def wait(self, eng, sem, v):
        if v <= 0:
            return
        key = (eng, id(sem))
        if self.lastw.get(key, -1) >= v:
            return
        self.lastw[key] = v
        self.q[eng].append(lambda E, h=sem.h, v=v: E.wait_ge(h, v))


class Pool:

    def __init__(self, prog, slots):
        self.prog = prog
        self.slots = [{"ap": s, "free": None} for s in slots]
        self.i = 0

    def acquire(self):
        s = self.slots[self.i % len(self.slots)]
        self.i += 1
        if s["free"] is not None:
            self.prog.wait("tensor", s["free"][0], s["free"][1])
        return s


def build(taps=False):
    nc = bass.Bass("TRN2", target_bir_lowering=False)

    dX = nc.dram_tensor("x", [S, D], F32, kind="ExternalInput").ap()
    dLNG = nc.dram_tensor("ln_g", [1, D], F32, kind="ExternalInput").ap()
    dLNB = nc.dram_tensor("ln_b", [1, D], F32, kind="ExternalInput").ap()
    dMIXW = nc.dram_tensor("mix_w", [D, 3], F32, kind="ExternalInput").ap()
    dMIXB = nc.dram_tensor("mix_b", [1, D], F32, kind="ExternalInput").ap()
    dWI = nc.dram_tensor("Wi", [D, D], F32, kind="ExternalInput").ap()
    dBI = nc.dram_tensor("bi", [1, D], F32, kind="ExternalInput").ap()
    dWSIN = nc.dram_tensor("Wsin", [D, SD], F32, kind="ExternalInput").ap()
    dBSIN = nc.dram_tensor("bsin", [1, SD], F32, kind="ExternalInput").ap()
    dWD = nc.dram_tensor("Wd", [D, SD], F32, kind="ExternalInput").ap()
    dBD = nc.dram_tensor("bd", [1, SD], F32, kind="ExternalInput").ap()
    dWSEL = nc.dram_tensor("Wsel", [D, SD], F32, kind="ExternalInput").ap()
    dBSEL = nc.dram_tensor("bsel", [1, SD], F32, kind="ExternalInput").ap()
    dWSO = nc.dram_tensor("Wso", [SD, D], F32, kind="ExternalInput").ap()
    dBSO = nc.dram_tensor("bso", [1, D], F32, kind="ExternalInput").ap()
    dWG = nc.dram_tensor("Wg", [D, D], F32, kind="ExternalInput").ap()
    dBG = nc.dram_tensor("bg", [1, D], F32, kind="ExternalInput").ap()
    dWOUT = nc.dram_tensor("Wout", [D, D], F32, kind="ExternalInput").ap()
    dBOUT = nc.dram_tensor("bout", [1, D], F32, kind="ExternalInput").ap()
    dWRED = nc.dram_tensor("w_red", [D, 16], F32, kind="ExternalInput").ap()
    dWG8 = nc.dram_tensor("wg8", [8, 16], F32, kind="ExternalInput").ap()
    dW1 = nc.dram_tensor("w1", [NE, D, FF], F32, kind="ExternalInput").ap()
    dB1 = nc.dram_tensor("b1", [NE, FF], F32, kind="ExternalInput").ap()
    dW2 = nc.dram_tensor("w2", [NE, FF, D], F32, kind="ExternalInput").ap()
    dB2 = nc.dram_tensor("b2", [NE, D], F32, kind="ExternalInput").ap()
    dIDENT = nc.dram_tensor("ident", [128, 128], F32, kind="ExternalInput").ap()
    dIDENTB = nc.dram_tensor("identb", [8, 8], BF16, kind="ExternalInput").ap()
    dE8 = nc.dram_tensor("e8", [8, 1024], BF16, kind="ExternalInput").ap()
    dONESF = nc.dram_tensor("ones_f", [1, 128], F32, kind="ExternalInput").ap()
    dONESB = nc.dram_tensor("ones_b", [1, TW], BF16, kind="ExternalInput").ap()
    dONESC = nc.dram_tensor("ones_c", [128, 1], BF16, kind="ExternalInput").ap()
    dZEROC = nc.dram_tensor("zero_c", [128, 1], F32, kind="ExternalInput").ap()

    dY = nc.dram_tensor("y", [S, D], F32, kind="ExternalOutput").ap()

    tap_t = {}
    if taps:
        for name, shape in [
            ("tA", [128, NK * S]), ("tH", [128, NK * SP]), ("tMIX", [128, NK * TW]),
            ("tDEC", [128, 2 * TW]), ("tST", [128, 2 * TW]), ("tU", [128, NK * TW]),
            ("tG", [128, NK * TW]), ("tPA", [128, NK * S]), ("tQ", [128, NK * S]),
            ("tMS", [8, S]), ("tYA", [128, NK * S]), ("tSB", [128, NK * S]),
            ("tRED", [16, S]), ("tLG", [8, S]), ("tSCR", [128, 16]),
        ]:
            tap_t[name] = nc.dram_tensor(name, shape, F32, kind="ExternalOutput").ap()

    P = Prog()
    es = contextlib.ExitStack()
    with es:
        def mksb(stack):
            return lambda name, shape, dt: stack.enter_context(nc.sbuf_tensor(name, shape, dt))

        sb = mksb(es)

        Pb = sb("Pb", [128, NK * S], BF16)
        ident = sb("ident_sb", [128, 128], F32)
        identb = sb("identb_sb", [8, 8], BF16)
        e8 = sb("e8_sb", [8, 1024], BF16)
        ones_f = sb("ones_f_sb", [1, 128], F32)
        ones_c = sb("ones_c_sb", [128, 1], BF16)
        zero_c = sb("zero_c_sb", [128, 1], F32)
        mu_r = sb("mu_r", [1, TW], F32)
        e2e_r = sb("e2e_r", [1, TW], F32)
        rstd_r = sb("rstd_r", [1, TW], F32)
        nrm_s = sb("nrm_s", [128, TW], F32)
        sq_bf = sb("sq_bf", [128, 2 * TW], BF16)
        lng_c = sb("lng_c", [128, NK], F32)
        lnb_c = sb("lnb_c", [128, NK], F32)
        mixw_c = sb("mixw_c", [128, NK * 3], F32)
        mixb_c = sb("mixb_c", [128, NK], F32)
        bi_c = sb("bi_c", [128, NK], F32)
        bg_c = sb("bg_c", [128, NK], F32)
        bout_c = sb("bout_c", [128, NK], F32)
        bso_c = sb("bso_c", [128, NK], F32)
        bsin_c = sb("bsin_c", [128, 2], F32)
        bd_c = sb("bd_c", [128, 2], F32)
        bsel_c = sb("bsel_c", [128, 2], F32)
        wg8_s = sb("wg8_s", [8, 16], F32)
        wred_b = sb("wred_b", [128, NK * 16], BF16)

        psA = es.enter_context(nc.psum_tensor("psA", [128, 1024], F32))
        psB = es.enter_context(nc.psum_tensor("psB", [128, 1024], F32))
        psC = es.enter_context(nc.psum_tensor("psC", [128, 1024], F32))
        psD = es.enter_context(nc.psum_tensor("psD", [128, 512], F32))
        psE = es.enter_context(nc.psum_tensor("psE", [128, 512], BF16))

        s_ld = Sem(es.enter_context(nc.semaphore("s_ld")))
        s_ls = Sem(es.enter_context(nc.semaphore("s_ls")))
        s_gl = Sem(es.enter_context(nc.semaphore("s_gl")))
        s_gp = Sem(es.enter_context(nc.semaphore("s_gp")))
        s_x = [Sem(es.enter_context(nc.semaphore(f"s_x{i}"))) for i in range(3)]
        s_w1 = [Sem(es.enter_context(nc.semaphore(f"s_w1{i}"))) for i in range(3)]
        s_w2 = [Sem(es.enter_context(nc.semaphore(f"s_w2{i}"))) for i in range(2)]
        s_b = Sem(es.enter_context(nc.semaphore("s_b")))
        s_o = [Sem(es.enter_context(nc.semaphore(f"s_o{i}"))) for i in range(2)]
        s_pe = Sem(es.enter_context(nc.semaphore("s_pe")))
        s_sc = Sem(es.enter_context(nc.semaphore("s_sc")))
        s_ve = Sem(es.enter_context(nc.semaphore("s_ve")))
        s_st = Sem(es.enter_context(nc.semaphore("s_st")))

        pool = Pool(P, [psA[:, 0:512], psA[:, 512:1024],
                        psB[:, 0:512], psB[:, 512:1024]])

        def sdma(dst, src):
            v = s_ld.inc(16)
            P.do("sync", lambda E, d=dst, s=src, h=s_ld.h: E.dma_start(d, s).then_inc(h, 16))
            return v

        def gdma(dst, src):
            v = s_gl.inc(16)
            P.do("gpsimd", lambda E, d=dst, s=src, h=s_gl.h: E.dma_start(d, s).then_inc(h, 16))
            return v

        def scdma(dst, src):
            v = s_ls.inc(16)
            P.do("scalar", lambda E, d=dst, s=src, h=s_ls.h: E.dma_start(d, s).then_inc(h, 16))
            return v

        def tap_dump(name, src_ap, wait_sem, wait_v):
            if not taps:
                return
            P.wait("gpsimd", wait_sem, wait_v)
            s_st.inc(16)
            P.do("gpsimd", lambda E, d=tap_t[name][:, :], s=src_ap, h=s_st.h:
                 E.dma_start(d, s).then_inc(h, 16))

        v_const = 0
        for dst, src in [(ident, dIDENT), (identb, dIDENTB), (e8, dE8), (ones_f, dONESF),
                         (ones_c, dONESC), (zero_c, dZEROC),
                         (wg8_s, dWG8)]:
            v_const = sdma(dst[:, :], src[:, :])
        v_mixw = sdma(mixw_c[:, :].rearrange("p (c j) -> p c j", c=NK),
                      dMIXW.rearrange("(c p) j -> p c j", p=128))
        v_wred = gdma(wred_b[:, :].rearrange("p (c m) -> p c m", c=NK),
                      dWRED.rearrange("(c p) m -> p c m", p=128))

        es_attn = es.enter_context(contextlib.ExitStack())
        sbA = mksb(es_attn)
        xstage = sbA("xstage", [128, 3 * 1024], F32)
        wi_b = sbA("wi_b", [128, NK * 1024], BF16)
        wg_b = sbA("wg_b", [128, NK * 1024], BF16)
        wout_b = sbA("wout_b", [128, NK * 1024], BF16)
        wsin_b = sbA("wsin_b", [128, NK * SD], BF16)
        wd_b = sbA("wd_b", [128, NK * SD], BF16)
        wsel_b = sbA("wsel_b", [128, NK * SD], BF16)
        wso_b = sbA("wso_b", [128, 2 * 1024], BF16)
        h_pad = sbA("h_pad", [128, NK * SP], BF16)
        mixed_bf = sbA("mixed_bf", [128, 2 * NK * TW], BF16)
        u_bf = sbA("u_bf", [128, NK * TW], BF16)
        g_bf = sbA("g_bf", [128, NK * TW], BF16)
        dec_f = sbA("dec_f", [128, 2 * TW], F32)
        xst_f = sbA("xst_f", [128, 2 * TW], F32)
        sel_f = sbA("sel_f", [128, 2 * TW], BF16)
        st_f = sbA("st_f", [128, 2 * TW], F32)
        ss_bf = sbA("ss_bf", [128, 2 * TW], BF16)
        carry = sbA("carry", [128, 2], F32)
        s0_bf = sbA("s0_bf", [128, TW], BF16)
        ctmp = sbA("ctmp", [128, TW], BF16)

        v_w = 0
        for dst, src, w in [(wsin_b, dWSIN, SD), (wd_b, dWD, SD), (wsel_b, dWSEL, SD)]:
            v_w = gdma(dst[:, :].rearrange("p (c m) -> p c m", c=NK),
                       src.rearrange("(c p) m -> p c m", p=128))
        v_w = gdma(wso_b[:, :].rearrange("p (c m) -> p c m", c=2),
                   dWSO.rearrange("(c p) m -> p c m", p=128))
        for dst, src in [(wi_b, dWI), (wg_b, dWG), (wout_b, dWOUT)]:
            v_w = gdma(dst[:, :].rearrange("p (c m) -> p c m", c=NK),
                       src.rearrange("(c p) m -> p c m", p=128))
        v_attn_w = v_w

        P.wait("tensor", s_ld, v_const)

        def row_to_cols(row_of, n, dst):
            def t_ops(E, srcs=[row_of(i) for i in range(n)]):
                for i, src in enumerate(srcs):
                    ins = E.transpose(psD[:, i:i + 1], src, ident[0:1, 0:1])
                return ins
            P.do("tensor", lambda E, f=t_ops: f(E).then_inc(s_pe.h, 1))
            pv = s_pe.inc()
            P.wait("scalar", s_pe, pv)
            P.do("scalar", lambda E, d=dst, n=n: E.copy(d, psD[:, 0:n]).then_inc(s_sc.h, 1))
            return s_sc.inc()

        row_waves = [
            [(dLNG, lng_c, NK), (dLNB, lnb_c, NK), (dMIXB, mixb_c, NK)],
            [(dBI, bi_c, NK), (dBG, bg_c, NK), (dBOUT, bout_c, NK)],
            [(dBSO, bso_c, NK), (dBSIN, bsin_c, 2), (dBD, bd_c, 2),
             (dBSEL, bsel_c, 2)],
        ]
        prev_cols = 0
        wave_pe = 0
        for wave in row_waves:
            P.wait("sync", s_pe, wave_pe)
            off = 0
            offs = []
            v_wave = 0
            for src, dst, n in wave:
                offs.append(off)
                v_wave = sdma(xstage[0:1, off:off + n * 128], src[0:1, :])
                off += n * 128
            P.wait("tensor", s_ld, v_wave)
            for (src, dst, n), o in zip(wave, offs):
                P.wait("tensor", s_sc, prev_cols)
                prev_cols = row_to_cols(
                    lambda j, o=o: xstage[0:1, o + j * 128:o + (j + 1) * 128],
                    n, dst[:, 0:n])
            wave_pe = s_pe.n
        v_cols = prev_cols
        v_bias_cols = prev_cols

        sqslot_free = {}
        ln_state = {"prev": 0}

        def emit_ln(src_ap, dst_ap, gain, out_chunks=NK, tiles=None):
            if tiles is None:
                tiles = range(NT)
            ln_done = 0
            for t in tiles:
                P.wait("tensor", s_ve, ln_state["prev"])
                pv = 0
                for k in range(NK):
                    slot = k % 2
                    if slot in sqslot_free:
                        P.wait("scalar", s_pe, sqslot_free[slot])
                    P.do("scalar", lambda E, k=k, t=t, slot=slot: E.activation(
                        sq_bf[:, slot * TW:(slot + 1) * TW], src_ap(k, t),
                        AF.Square).then_inc(s_sc.h, 1))
                    sv = s_sc.inc()
                    P.do("tensor", lambda E, k=k, t=t: E.matmul(
                        psC[0:1, 0:TW], ones_c[:, :], src_ap(k, t),
                        start=(k == 0), stop=(k == NK - 1)))
                    P.wait("tensor", s_sc, sv)
                    P.do("tensor", lambda E, k=k, slot=slot: E.matmul(
                        psC[0:1, 512:512 + TW], ones_c[:, :],
                        sq_bf[:, slot * TW:(slot + 1) * TW],
                        start=(k == 0), stop=(k == NK - 1)).then_inc(s_pe.h, 1))
                    pv = s_pe.inc()
                    sqslot_free[slot] = pv
                v_stats = pv
                P.wait("vector", s_pe, v_stats)
                P.do("vector", lambda E: E.tensor_scalar(mu_r[:, :], psC[0:1, 0:TW],
                                                         1.0 / D, None, ALU.mult))
                P.do("vector", lambda E: E.tensor_scalar(e2e_r[:, :], psC[0:1, 512:512 + TW],
                                                         1.0 / D, EPS, ALU.mult, ALU.add))
                P.do("vector", lambda E: E.tensor_tensor(rstd_r[:, :], mu_r[:, :], mu_r[:, :],
                                                         ALU.mult))
                P.do("vector", lambda E: E.tensor_tensor(e2e_r[:, :], e2e_r[:, :], rstd_r[:, :],
                                                         ALU.subtract).then_inc(s_ve.h, 1))
                vv = s_ve.inc()
                P.wait("scalar", s_ve, vv)
                P.do("scalar", lambda E: E.activation(rstd_r[:, :], e2e_r[:, :],
                                                      AF.Ln).then_inc(s_sc.h, 1))
                P.wait("scalar", s_sc, s_sc.inc())
                P.do("scalar", lambda E: E.activation(e2e_r[:, :], rstd_r[:, :],
                                                      AF.Exp, scale=-0.5).then_inc(s_sc.h, 1))
                sv2 = s_sc.inc()
                P.wait("vector", s_sc, sv2)
                P.do("vector", lambda E: E.tensor_tensor(mu_r[:, :], mu_r[:, :], e2e_r[:, :],
                                                         ALU.mult).then_inc(s_ve.h, 1))
                vv2 = s_ve.inc()
                P.wait("tensor", s_ve, vv2)
                P.do("tensor", lambda E: E.matmul(psC[:, 0:TW], ones_f[0:1, 0:128],
                                                  e2e_r[:, :], start=True, stop=True))
                P.do("tensor", lambda E: E.matmul(psC[:, 512:512 + TW], ones_f[0:1, 0:128],
                                                  mu_r[:, :], start=True,
                                                  stop=True).then_inc(s_pe.h, 1))
                v_bc = s_pe.inc()
                P.wait("vector", s_pe, v_bc)
                for k in range(out_chunks):
                    P.do("vector", lambda E, k=k, t=t: E.tensor_tensor(
                        nrm_s[:, :], src_ap(k, t), psC[:, 0:TW], ALU.mult))
                    if gain:
                        P.do("vector", lambda E, k=k, t=t: E.tensor_tensor(
                            nrm_s[:, :], nrm_s[:, :], psC[:, 512:512 + TW], ALU.subtract))
                        P.do("vector", lambda E, k=k, t=t: E.tensor_scalar(
                            dst_ap(k, t), nrm_s[:, :], lng_c[:, k:k + 1], lnb_c[:, k:k + 1],
                            ALU.mult, ALU.add))
                    else:
                        P.do("vector", lambda E, k=k, t=t: E.tensor_tensor(
                            dst_ap(k, t), nrm_s[:, :], psC[:, 512:512 + TW], ALU.subtract))
                P.do("vector", lambda E: E.engine_nop().then_inc(s_ve.h, 1))
                ln_done = s_ve.inc()
                ln_state["prev"] = ln_done
            return ln_done

        A_src = lambda k, t: Pb[:, k * S + t * TW: k * S + (t + 1) * TW]
        h_dst = lambda k, t: h_pad[:, k * SP + 1 + t * TW: k * SP + 1 + (t + 1) * TW]

        for k in range(NK):
            P.do("vector", lambda E, k=k: E.memset(h_pad[:, k * SP:k * SP + 1], 0.0))
            P.do("vector", lambda E, k=k: E.memset(
                h_pad[:, k * SP + 1 + S:k * SP + 2 + S], 0.0))
        P.do("vector", lambda E: E.engine_nop().then_inc(s_ve.h, 1))
        s_ve.inc()
        P.do("scalar", lambda E: E.memzero(carry[:, :]).then_inc(s_sc.h, 1))
        v_carry = s_sc.inc()

        MW = NK * TW
        mix_rhs = lambda k, par: mixed_bf[:, par * MW + k * TW: par * MW + (k + 1) * TW]
        cpe_hist = [0, 0]
        conv_done = {}

        def emit_conv(t):
            par = t % 2
            if t >= 2:
                P.wait("vector", s_pe, cpe_hist[par])
            for k in range(NK):
                base = k * SP + t * TW
                P.do("vector", lambda E, k=k, base=base: E.tensor_scalar(
                    s0_bf[:, 0:TW], h_pad[:, base: base + TW],
                    mixw_c[:, k * 3:k * 3 + 1], mixb_c[:, k:k + 1],
                    ALU.mult, ALU.add))
                P.do("vector", lambda E, k=k, base=base: E.scalar_tensor_tensor(
                    ctmp[:, 0:TW], h_pad[:, base + 1: base + 1 + TW],
                    mixw_c[:, k * 3 + 1:k * 3 + 2], s0_bf[:, 0:TW],
                    ALU.mult, ALU.add))
                def g3(E, k=k, base=base, par=par):
                    return E.scalar_tensor_tensor(
                        mix_rhs(k, par), h_pad[:, base + 2: base + 2 + TW],
                        mixw_c[:, k * 3 + 2:k * 3 + 3], ctmp[:, 0:TW],
                        ALU.mult, ALU.add)
                P.do("vector", lambda E, f=g3: f(E).then_inc(s_ve.h, 1))
                s_ve.inc()
            conv_done[t] = s_ve.n

        P.wait("sync", s_pe, wave_pe)
        P.wait("scalar", s_pe, wave_pe)
        xs_free = [None, None, None]
        ln1_marks = {}
        for g in range(16):
            slot = g % 3
            q = "sync" if g % 2 == 0 else "scalar"
            if xs_free[slot] is not None:
                P.wait(q, xs_free[slot][0], xs_free[slot][1])
            dstx = xstage[:, slot * 1024:(slot + 1) * 1024]
            srcx = dX[g * 128:(g + 1) * 128, :]
            vx = s_x[slot].inc(16)
            P.do(q, lambda E, d=dstx, s=srcx, h=s_x[slot].h:
                 E.dma_start(d, s).then_inc(h, 16))
            P.wait("tensor", s_x[slot], vx)
            for half in range(2):
                sl = pool.acquire()
                def t_tr(E, sl=sl, slot=slot, half=half):
                    for j in range(4):
                        k = half * 4 + j
                        ins = E.transpose(sl["ap"][:, j * 128:(j + 1) * 128],
                                          xstage[:, slot * 1024 + k * 128:
                                                 slot * 1024 + (k + 1) * 128],
                                          ident[:, :])
                    return ins
                P.do("tensor", lambda E, f=t_tr: f(E).then_inc(s_pe.h, 1))
                pv = s_pe.inc()
                P.wait("scalar", s_pe, pv)
                out_ap = Pb[:, :].rearrange("p (c s) -> p c s", c=NK)[
                    :, half * 4:(half + 1) * 4, g * 128:(g + 1) * 128]
                P.do("scalar", lambda E, o=out_ap, sl=sl: E.copy(
                    o, sl["ap"][:, 0:512].rearrange("p (c s) -> p c s", c=4)).then_inc(s_sc.h, 1))
                sv = s_sc.inc()
                sl["free"] = (s_sc, sv)
                if half == 1:
                    xs_free[slot] = (s_sc, sv)
            if g % 4 == 3:
                t_ln = g // 4
                xmark = s_sc.n
                P.wait("tensor", s_sc, xmark)
                P.wait("vector", s_sc, xmark)
                ln1_marks[t_ln] = emit_ln(A_src, h_dst, gain=True, tiles=[t_ln])
                if t_ln in (1, 2):
                    emit_conv(t_ln - 1)
        v_A_done = s_sc.n
        v_ln1 = ln1_marks[NT - 1]
        tap_dump("tA", Pb[:, :], s_sc, v_A_done)
        tap_dump("tH", h_pad[:, :], s_ve, v_ln1)

        P.wait("tensor", s_gl, v_attn_w)
        P.wait("scalar", s_ve, v_ln1)
        prev = {"dxs_ve": 0, "sel_ve": 0, "g_ve": 0}
        ss_rhs = lambda k: ss_bf[:, k * TW:(k + 1) * TW]

        def emit_proj(wbuf, wwidth, bcol, n_k, m, act, dst_ap, v_rhs, par):
            sl = pool.acquire()
            P.wait("tensor", s_ve, v_rhs)
            def mmf(E, sl=sl, wbuf=wbuf, wwidth=wwidth, n_k=n_k, m=m, par=par):
                for k in range(n_k):
                    ins = E.matmul(sl["ap"],
                                   wbuf[:, k * wwidth + m * 128: k * wwidth + (m + 1) * 128],
                                   mix_rhs(k, par), start=(k == 0), stop=(k == n_k - 1))
                return ins
            P.do("tensor", lambda E, f=mmf: f(E).then_inc(s_pe.h, 1))
            pv = s_pe.inc()
            P.wait("scalar", s_pe, pv)
            P.do("scalar", lambda E, sl=sl, act=act, dst=dst_ap, bcol=bcol: E.activation(
                dst, sl["ap"], act, bias=bcol, scale=1.0).then_inc(s_sc.h, 1))
            sv = s_sc.inc()
            sl["free"] = (s_sc, sv)
            return sv

        for t in range(NT):
            par = t % 2
            v_mix = conv_done[t]

            P.wait("scalar", s_ve, prev["dxs_ve"])
            v_xst = [emit_proj(wsin_b, SD, bsin_c[:, m:m + 1], NK, m, AF.Tanh,
                               xst_f[:, m * TW:(m + 1) * TW], v_mix, par) for m in range(2)]
            v_dec = [emit_proj(wd_b, SD, bd_c[:, m:m + 1], NK, m, AF.Sigmoid,
                               dec_f[:, m * TW:(m + 1) * TW], v_mix, par) for m in range(2)]
            P.wait("vector", s_sc, v_dec[1])
            P.wait("vector", s_sc, v_carry)
            for c in range(2):
                P.do("vector", lambda E, c=c: E.tensor_tensor(
                    nrm_s[:, :], dec_f[:, c * TW:(c + 1) * TW],
                    xst_f[:, c * TW:(c + 1) * TW], ALU.mult))
                P.do("vector", lambda E, c=c: E.tensor_tensor(
                    xst_f[:, c * TW:(c + 1) * TW], xst_f[:, c * TW:(c + 1) * TW],
                    nrm_s[:, :], ALU.subtract))
                P.do("vector", lambda E, c=c: E.tensor_tensor_scan(
                    st_f[:, c * TW:(c + 1) * TW], dec_f[:, c * TW:(c + 1) * TW],
                    xst_f[:, c * TW:(c + 1) * TW], carry[:, c:c + 1], ALU.mult, ALU.add))
            P.do("vector", lambda E: E.engine_nop().then_inc(s_ve.h, 1))
            v_scan = s_ve.inc()
            prev["dxs_ve"] = v_scan
            P.wait("scalar", s_ve, v_scan)
            P.do("scalar", lambda E: E.copy(carry[:, 0:1], st_f[:, TW - 1: TW]))
            P.do("scalar", lambda E: E.copy(carry[:, 1:2],
                                            st_f[:, 2 * TW - 1: 2 * TW]).then_inc(s_sc.h, 1))
            v_carry = s_sc.inc()

            P.wait("scalar", s_ve, prev["g_ve"])
            v_u = [emit_proj(wi_b, 1024, bi_c[:, m:m + 1], NK, m, AF.Tanh,
                             u_bf[:, m * TW:(m + 1) * TW], v_mix, par) for m in range(NK)]
            P.wait("scalar", s_ve, prev["sel_ve"])
            v_sel = [emit_proj(wsel_b, SD, bsel_c[:, m:m + 1], NK, m, AF.Sigmoid,
                               sel_f[:, m * TW:(m + 1) * TW], v_mix, par) for m in range(2)]
            v_g = [emit_proj(wg_b, 1024, bg_c[:, m:m + 1], NK, m, AF.Sigmoid,
                             g_bf[:, m * TW:(m + 1) * TW], v_mix, par) for m in range(NK)]
            v_mix_consumed = s_pe.n

            if t == 0:
                tap_dump("tMIX", mixed_bf[:, 0:MW], s_pe, v_mix_consumed)
                tap_dump("tDEC", dec_f[:, :], s_ve, v_scan)
                tap_dump("tST", st_f[:, :], s_ve, v_scan)

            P.wait("vector", s_sc, v_g[NK - 1])
            for k in range(NK):
                P.do("vector", lambda E, k=k: E.tensor_tensor(
                    ctmp[:, 0:TW], u_bf[:, k * TW:(k + 1) * TW],
                    g_bf[:, k * TW:(k + 1) * TW], ALU.mult))
                P.do("vector", lambda E, k=k: E.tensor_tensor(
                    u_bf[:, k * TW:(k + 1) * TW], u_bf[:, k * TW:(k + 1) * TW],
                    ctmp[:, 0:TW], ALU.subtract))
            P.do("vector", lambda E: E.tensor_tensor(ss_bf[:, :], sel_f[:, :], st_f[:, :],
                                                     ALU.mult))
            P.do("vector", lambda E: E.engine_nop().then_inc(s_ve.h, 1))
            v_ss = s_ve.inc()
            prev["sel_ve"] = v_ss

            if t == 0:
                tap_dump("tU", u_bf[:, :], s_ve, v_ss)
                tap_dump("tG", g_bf[:, :], s_sc, v_g[NK - 1])

            v_c_ve = 0
            for m in range(NK):
                sl = pool.acquire()
                P.wait("tensor", s_ve, v_ss)
                def mmso(E, sl=sl, m=m):
                    for kc in range(2):
                        ins = E.matmul(sl["ap"],
                                       wso_b[:, kc * 1024 + m * 128: kc * 1024 + (m + 1) * 128],
                                       ss_rhs(kc), start=(kc == 0), stop=(kc == 1))
                    return ins
                P.do("tensor", lambda E, f=mmso: f(E).then_inc(s_pe.h, 1))
                pv = s_pe.inc()
                P.wait("vector", s_pe, max(pv, v_mix_consumed))
                P.do("vector", lambda E, sl=sl, m=m: E.scalar_tensor_tensor(
                    ctmp[:, 0:TW], sl["ap"],
                    bso_c[:, m:m + 1], g_bf[:, m * TW:(m + 1) * TW],
                    ALU.add, ALU.mult))
                P.do("vector", lambda E, sl=sl, m=m, par=par: E.tensor_tensor(
                    mix_rhs(m, par), ctmp[:, 0:TW],
                    u_bf[:, m * TW:(m + 1) * TW], ALU.add).then_inc(s_ve.h, 1))
                v_c_ve = s_ve.inc()
                sl["free"] = (s_ve, v_c_ve)
            prev["g_ve"] = v_c_ve

            for m in range(NK):
                sl = pool.acquire()
                P.wait("tensor", s_ve, v_c_ve)
                def mmout(E, sl=sl, m=m, par=par):
                    for k in range(NK):
                        ins = E.matmul(sl["ap"],
                                       wout_b[:, k * 1024 + m * 128: k * 1024 + (m + 1) * 128],
                                       mix_rhs(k, par), start=(k == 0), stop=(k == NK - 1))
                    return ins
                P.do("tensor", lambda E, f=mmout: f(E).then_inc(s_pe.h, 1))
                pv = s_pe.inc()
                P.wait("vector", s_pe, pv)
                P.do("vector", lambda E, sl=sl, m=m, t=t: E.scalar_tensor_tensor(
                    A_src(m, t), A_src(m, t), 2.0, sl["ap"], ALU.mult, ALU.add))
                P.do("vector", lambda E, m=m, t=t: E.tensor_scalar(
                    A_src(m, t), A_src(m, t), bout_c[:, m:m + 1], None,
                    ALU.add).then_inc(s_ve.h, 1))
                v_res = s_ve.inc()
                sl["free"] = (s_ve, v_res)
            cpe_hist[par] = s_pe.n
            if t + 2 < NT:
                emit_conv(t + 2)
        v_attn_done = s_ve.n
        v_attn_pe = s_pe.n
        tap_dump("tPA", Pb[:, :], s_ve, v_attn_done)

        es_attn.close()
        es_post = es.enter_context(contextlib.ExitStack())
        sbP = mksb(es_post)
        Qb = sbP("Qb", [128, NK * S], BF16)
        hfc = sbP("hfc", [128, NMF * S], BF16)
        ms_all = sbP("ms_all", [8, S], BF16)
        w1g = sbP("w1g", [128, 3 * NK * 256], BF16)
        w2g = sbP("w2g", [128, 2 * 16 * 256], BF16)
        ms_bc = sbP("ms_bc", [128, 2 * S], BF16)
        tmp_bf = sbP("tmp_bf", [128, TW], BF16)
        b1r_f = sbP("b1r_f", [1, FCW], BF16)
        b1c = sbP("b1c", [128, 64], F32)
        b2c = sbP("b2c", [128, 16], F32)

        Q_dst = lambda k, t: Qb[:, k * S + t * TW: k * S + (t + 1) * TW]
        P.wait("tensor", s_ve, v_attn_done)
        P.wait("scalar", s_ve, v_attn_done)
        v_ln2 = emit_ln(A_src, Q_dst, gain=False)
        tap_dump("tQ", Qb[:, :], s_ve, v_ln2)

        P.wait("gpsimd", s_ve, v_ln2)
        P.wait("gpsimd", s_pe, v_attn_pe)
        P.wait("gpsimd", s_sc, s_sc.n)
        w1_free = [None, None, None]
        w2_free = [None, None]

        def bdma(dst, srcap):
            v = s_b.inc(16)
            P.do("gpsimd", lambda E, d=dst, s=srcap, h=s_b.h:
                 E.dma_start(d, s).then_inc(h, 16))
            return v
        v_b1h0 = {0: bdma(b1r_f[:, :], dB1[0:1, 0:FCW])}
        v_b1h1 = {}
        v_b2r = {}

        msbc_half_prev = {0: 0, 1: 0}
        rt = {}
        msbc_done = {}
        bprep_done = {}
        bprep_prev = 0
        v_y_end = {}

        def emit_msbc(e, ts):
            par = e % 2
            if e >= 2:
                P.wait("scalar", s_ve, v_y_end[e - 2])
            for t in ts:
                half = t % 2
                P.wait("tensor", s_sc, msbc_half_prev[half])
                P.do("tensor", lambda E, e=e, t=t, half=half: E.matmul(
                    psC[:, half * 512: half * 512 + TW], e8[0:8, e * 128:(e + 1) * 128],
                    ms_all[:, t * TW:(t + 1) * TW], start=True,
                    stop=True).then_inc(s_pe.h, 1))
                pv = s_pe.inc()
                P.wait("scalar", s_pe, pv)
                P.do("scalar", lambda E, par=par, t=t, half=half: E.copy(
                    ms_bc[:, par * S + t * TW: par * S + (t + 1) * TW],
                    psC[:, half * 512: half * 512 + TW]).then_inc(s_sc.h, 1))
                msbc_half_prev[half] = s_sc.inc()
            if ts[-1] == NT - 1:
                msbc_done[e] = s_sc.n

        def emit_bprep(e):
            par = e % 2
            global_b = bprep_prev
            P.wait("tensor", s_b, v_b1h0[e])
            P.wait("tensor", s_sc, global_b)
            if e >= 2:
                P.wait("scalar", s_ve, v_y_end[e - 2])
            P.wait("tensor", s_sc, rt.get("lgtm", 0))
            def mmtr_b1(E, off):
                for i in range(16):
                    c = 2 * (off + i)
                    ins = E.transpose(psE[:, c:c + 1],
                                      b1r_f[0:1, i * 128:(i + 1) * 128],
                                      identb[0:1, 0:1])
                return ins
            P.do("tensor", lambda E: mmtr_b1(E, 0).then_inc(s_pe.h, 1))
            pv1 = s_pe.inc()
            P.wait("gpsimd", s_pe, pv1)
            v_b1h1[e] = bdma(b1r_f[:, :], dB1[e:e + 1, FCW:FF])
            P.wait("tensor", s_b, v_b1h1[e])
            P.do("tensor", lambda E: mmtr_b1(E, 16).then_inc(s_pe.h, 1))
            pv2 = s_pe.inc()
            P.wait("gpsimd", s_pe, pv2)
            v_b2r[e] = bdma(b1r_f[0:1, 0:D], dB2[e:e + 1, :])
            P.wait("tensor", s_b, v_b2r[e])
            def mmtr_b2(E):
                for i in range(8):
                    ins = E.transpose(psE[:, 64 + 2 * i:65 + 2 * i],
                                      b1r_f[0:1, i * 128:(i + 1) * 128],
                                      identb[0:1, 0:1])
                return ins
            P.do("tensor", lambda E, f=mmtr_b2: f(E).then_inc(s_pe.h, 1))
            pv_b = s_pe.inc()
            P.wait("scalar", s_pe, pv_b)
            ev2 = psE[:, :].rearrange("p (i two) -> p i two", two=2)[:, :, 0]
            P.do("scalar", lambda E, par=par, ev2=ev2: E.copy(
                b1c[:, par * 32:(par + 1) * 32], ev2[:, 0:32]))
            P.do("scalar", lambda E, par=par, ev2=ev2: E.copy(
                b2c[:, par * 8:(par + 1) * 8], ev2[:, 32:40]).then_inc(s_sc.h, 1))
            bprep_done[e] = s_sc.inc()
            if e + 1 < NE:
                P.wait("gpsimd", s_pe, pv_b)
                v_b1h0[e + 1] = bdma(b1r_f[:, :], dB1[e + 1:e + 2, 0:FCW])
            return bprep_done[e]


        es_rt = es.enter_context(contextlib.ExitStack())
        sbR = mksb(es_rt)
        red_f = sbR("red_f", [16, S], BF16)
        lg_f = sbR("lg_f", [8, S], BF16)
        lgtm = sbR("lgtm", [128, 128], F32)
        mstm = sbR("mstm", [128, 128], F32)
        mx_c = sbR("mx_c", [128, 16], F32)
        ssum = sbR("ssum", [128, 16], F32)
        scrs = sbR("scrs", [128, 16], F32)
        exp_s = sbR("exp_s", [128, 8], F32)
        lgs_s = sbR("lgs_s", [128, 16], F32)
        wgT_f = sbR("wgT_f", [16, 8], BF16)
        nrm8 = sbR("nrm8", [8, 2], F32)
        rn8 = sbR("rn8", [8, 1], F32)
        wgsq = sbR("wgsq", [8, 16], F32)

        P.wait("tensor", s_gl, v_wred)
        P.wait("tensor", s_ve, v_ln2)
        red_ev = {}
        for t in range(NT):
            half = t % 2
            if t >= 2:
                P.wait("tensor", s_sc, red_ev[t - 2])
            def mmred(E, t=t, half=half):
                for k in range(NK):
                    ins = E.matmul(psC[0:16, half * 512: half * 512 + TW],
                                   wred_b[:, k * 16:(k + 1) * 16],
                                   Qb[:, k * S + t * TW: k * S + (t + 1) * TW],
                                   start=(k == 0), stop=(k == NK - 1))
                return ins
            P.do("tensor", lambda E, f=mmred: f(E).then_inc(s_pe.h, 1))
            pv = s_pe.inc()
            P.wait("scalar", s_pe, pv)
            P.do("scalar", lambda E, t=t, half=half: E.copy(
                red_f[:, t * TW:(t + 1) * TW],
                psC[0:16, half * 512: half * 512 + TW]).then_inc(s_sc.h, 1))
            red_ev[t] = s_sc.inc()
        v_red = red_ev[NT - 1]
        tap_dump("tRED", red_f[:, :], s_sc, v_red)

        P.wait("scalar", s_ld, v_const)
        P.do("scalar", lambda E: E.activation(wgsq[:, :], wg8_s[:, :],
                                              AF.Square).then_inc(s_sc.h, 1))
        v1 = s_sc.inc()
        P.wait("vector", s_sc, v1)
        P.do("vector", lambda E: E.reduce_sum(nrm8[:, 0:1], wgsq[:, :],
                                              axis=AX.X).then_inc(s_ve.h, 1))
        v2 = s_ve.inc()
        P.wait("scalar", s_ve, v2)
        P.do("scalar", lambda E: E.activation(nrm8[:, 1:2], nrm8[:, 0:1],
                                              AF.Sqrt).then_inc(s_sc.h, 1))
        v3 = s_sc.inc()
        P.wait("vector", s_sc, v3)
        P.do("vector", lambda E: E.tensor_single_scalar(nrm8[:, 1:2], nrm8[:, 1:2], 1e-4,
                                                        ALU.max).then_inc(s_ve.h, 1))
        v4 = s_ve.inc()
        P.wait("scalar", s_ve, v4)
        P.do("scalar", lambda E: E.activation(rn8[:, :], nrm8[:, 1:2],
                                              AF.Ln).then_inc(s_sc.h, 1))
        v4b = s_sc.inc()
        P.wait("vector", s_sc, v4b)
        P.do("vector", lambda E: E.tensor_scalar(rn8[:, :], rn8[:, :], -1.0, None,
                                                 ALU.mult).then_inc(s_ve.h, 1))
        v4c = s_ve.inc()
        P.wait("scalar", s_ve, v4c)
        P.do("scalar", lambda E: E.activation(rn8[:, :], rn8[:, :],
                                              AF.Exp).then_inc(s_sc.h, 1))
        v_rn8 = s_sc.inc()
        P.do("tensor", lambda E: E.transpose(psD[0:16, 0:8], wg8_s[:, :],
                                             ident[0:8, 0:8]).then_inc(s_pe.h, 1))
        v_wgT = s_pe.inc()
        P.wait("scalar", s_pe, v_wgT)
        P.do("scalar", lambda E: E.copy(wgT_f[:, :], psD[0:16, 0:8]).then_inc(s_sc.h, 1))
        v_wgTc = s_sc.inc()

        bprep_prev = v_wgTc
        bprep_prev = emit_bprep(0)

        P.wait("tensor", s_sc, bprep_done[0])
        P.wait("tensor", s_sc, v_red)
        P.wait("scalar", s_sc, v_rn8)
        lg_ev = {}
        for t in range(NT):
            if t >= 1:
                P.wait("tensor", s_ve, lg_ev[t - 1])
            P.do("tensor", lambda E, t=t: E.matmul(psD[0:8, 0:TW], wgT_f[:, :],
                                                   red_f[:, t * TW:(t + 1) * TW],
                                                   start=True, stop=True).then_inc(s_pe.h, 1))
            pv = s_pe.inc()
            P.wait("vector", s_pe, pv)
            P.wait("vector", s_sc, v_rn8)
            P.do("vector", lambda E, t=t: E.tensor_scalar(lg_f[:, t * TW:(t + 1) * TW],
                                                          psD[0:8, 0:TW], rn8[:, :], None,
                                                          ALU.mult).then_inc(s_ve.h, 1))
            lg_ev[t] = s_ve.inc()
        v_lg = lg_ev[NT - 1]
        tap_dump("tLG", lg_f[:, :], s_ve, v_lg)

        msbc_half_prev[0] = red_ev[2]
        msbc_half_prev[1] = red_ev[3]
        mx2 = sbR("mx2", [128, 16], F32)
        scr2 = sbR("scr2", [128, 16], F32)
        exp_done = {}
        ms_ev = {}

        def rt_lgT():
            P.wait("tensor", s_ve, v_lg)
            P.wait("tensor", s_sc, bprep_done[0])
            def mmtr_lg(E):
                for i in range(16):
                    ins = E.transpose(psE[:, i * 8:(i + 1) * 8],
                                      lg_f[0:8, i * 128:(i + 1) * 128], identb[0:8, 0:8])
                return ins
            P.do("tensor", lambda E: mmtr_lg(E).then_inc(s_pe.h, 1))
            v_lgT = s_pe.inc()
            P.wait("scalar", s_pe, v_lgT)
            P.do("scalar", lambda E: E.copy(lgtm[:, :], psE[:, 0:128]).then_inc(s_sc.h, 1))
            rt["lgtm"] = s_sc.inc()
            P.wait("vector", s_sc, rt["lgtm"])
            for i in range(16):
                P.do("vector", lambda E, i=i: E.reduce_max(
                    mx_c[:, i:i + 1], lgtm[:, i * 8:(i + 1) * 8], axis=AX.X))
            P.do("vector", lambda E: E.engine_nop().then_inc(s_ve.h, 1))
            v_mx = s_ve.inc()
            P.wait("scalar", s_ve, v_mx)
            P.do("scalar", lambda E: E.copy(mx2[:, :], mx_c[:, :]).then_inc(s_sc.h, 1))
            rt["mx2"] = s_sc.inc()

        def rt_exp(i0, i1):
            if i0 == 0:
                P.wait("vector", s_sc, rt["mx2"])
            for i in range(i0, i1):
                slot = i % 2
                if i >= 2:
                    P.wait("vector", s_sc, exp_done[i - 2])
                P.do("vector", lambda E, i=i, slot=slot: E.tensor_scalar(
                    lgs_s[:, slot * 8:(slot + 1) * 8], lgtm[:, i * 8:(i + 1) * 8],
                    mx2[:, i:i + 1], None, ALU.subtract).then_inc(s_ve.h, 1))
                vv = s_ve.inc()
                P.wait("scalar", s_ve, vv)
                P.do("scalar", lambda E, i=i, slot=slot: E.activation(
                    exp_s[:, :], lgs_s[:, slot * 8:(slot + 1) * 8], AF.Exp,
                    accum_out=ssum[:, i:i + 1]).then_inc(s_sc.h, 1))
                exp_done[i] = s_sc.inc()

        def rt_fin():
            v_esum = s_sc.n
            P.wait("vector", s_sc, v_esum)
            P.do("vector", lambda E: E.reciprocal(scrs[:, :], ssum[:, :]).then_inc(s_ve.h, 1))
            v_scr = s_ve.inc()
            P.wait("scalar", s_ve, v_scr)
            P.do("scalar", lambda E: E.copy(scr2[:, :], scrs[:, :]).then_inc(s_sc.h, 1))
            v_scr2 = s_sc.inc()
            P.wait("vector", s_sc, v_scr2)
            for i in range(16):
                P.do("vector", lambda E, i=i: E.tensor_scalar(
                    mstm[:, i * 8:(i + 1) * 8], lgtm[:, i * 8:(i + 1) * 8],
                    mx2[:, i:i + 1], None, ALU.is_equal))
                P.do("vector", lambda E, i=i: E.tensor_scalar(
                    mstm[:, i * 8:(i + 1) * 8], mstm[:, i * 8:(i + 1) * 8],
                    scr2[:, i:i + 1], None, ALU.mult))
            P.do("vector", lambda E: E.engine_nop().then_inc(s_ve.h, 1))
            rt["mstm"] = s_ve.inc()

        def rt_mstrans():
            P.wait("tensor", s_ve, rt["mstm"])
            for grp in range(4):
                if grp >= 1:
                    P.wait("tensor", s_sc, ms_ev[grp - 1])
                def mmtr_ms(E, grp=grp):
                    for j in range(4):
                        i = grp * 4 + j
                        ins = E.transpose(psD[0:8, j * 128:(j + 1) * 128],
                                          mstm[:, i * 8:(i + 1) * 8], ident[:, :])
                    return ins
                P.do("tensor", lambda E, f=mmtr_ms: f(E).then_inc(s_pe.h, 1))
                pv = s_pe.inc()
                P.wait("scalar", s_pe, pv)
                P.do("scalar", lambda E, grp=grp: E.copy(
                    ms_all[:, grp * 512:(grp + 1) * 512],
                    psD[0:8, 0:512]).then_inc(s_sc.h, 1))
                ms_ev[grp] = s_sc.inc()
            msbc_half_prev[0] = ms_ev[3]
            msbc_half_prev[1] = ms_ev[3]
            tap_dump("tMS", ms_all[:, :], s_sc, ms_ev[3])

        def rt_msbc():
            emit_msbc(0, [0, 1, 2, 3])

        rt_hooks = {1: rt_lgT, 2: lambda: rt_exp(0, 4), 3: lambda: rt_exp(4, 8),
                    4: lambda: rt_exp(8, 12),
                    5: lambda: (rt_exp(12, 16), rt_fin()),
                    6: rt_mstrans, 7: rt_msbc}

        es_rt.close()
        es_ex = es.enter_context(contextlib.ExitStack())
        sbE = mksb(es_ex)
        of32 = sbE("of32", [128, NK * TW], F32)
        out_tm = sbE("out_tm", [128, 2 * 512], F32)
        ot_free = [None, None]
        s_dst = lambda k, t: hfc[:, k * S + t * TW: k * S + (t + 1) * TW]
        of_dst = lambda k, tt: of32[:, k * TW:(k + 1) * TW]

        def emit_tail(t):
            for k in range(NK):
                P.do("vector", lambda E, k=k, t=t: E.tensor_tensor(
                    s_dst(k, t), Q_dst(k, t), A_src(k, t), ALU.add))
            P.do("vector", lambda E: E.engine_nop().then_inc(s_ve.h, 1))
            v_sb = s_ve.inc()
            P.wait("scalar", s_ve, v_sb)
            P.wait("tensor", s_ve, v_sb)
            P.wait("tensor", s_sc, s_sc.n)
            v_ln3 = emit_ln(s_dst, of_dst, gain=False, tiles=[t])
            P.wait("tensor", s_ve, v_ln3)
            for gj in range(4):
                g = t * 4 + gj
                for half in range(2):
                    sl = pool.acquire()
                    def t_tr(E, sl=sl, gj=gj, half=half):
                        for j in range(4):
                            k = half * 4 + j
                            ins = E.transpose(sl["ap"][:, j * 128:(j + 1) * 128],
                                              of32[:, k * TW + gj * 128: k * TW + (gj + 1) * 128],
                                              ident[:, :])
                        return ins
                    P.do("tensor", lambda E, f=t_tr: f(E).then_inc(s_pe.h, 1))
                    pv = s_pe.inc()
                    P.wait("scalar", s_pe, pv)
                    if ot_free[half] is not None:
                        P.wait("scalar", ot_free[half][0], ot_free[half][1])
                    P.do("scalar", lambda E, sl=sl, half=half: E.copy(
                        out_tm[:, half * 512:(half + 1) * 512],
                        sl["ap"]).then_inc(s_sc.h, 1))
                    sv = s_sc.inc()
                    sl["free"] = (s_sc, sv)
                    P.wait("sync", s_sc, sv)
                    stv = s_o[half].inc(16)
                    P.do("sync", lambda E, g=g, half=half, h=s_o[half].h: E.dma_start(
                        dY[g * 128:(g + 1) * 128, half * 512:(half + 1) * 512],
                        out_tm[:, half * 512:(half + 1) * 512]).then_inc(h, 16))
                    ot_free[half] = (s_o[half], stv)
            P.wait("vector", s_pe, s_pe.n)

        for e in range(NE):
            par_e = e % 2
            for fc in range(2):
                rt_hooks_here = rt_hooks if (e == 0 and fc == 0) else None
                def w1dma(gi, fc=fc, e=e):
                    slot = gi % 3
                    if w1_free[slot] is not None:
                        P.wait("gpsimd", s_pe, w1_free[slot])
                    src = dW1[e].rearrange("(k p) f -> p k f", p=128)[
                        :, :, fc * FCW + gi * 256: fc * FCW + (gi + 1) * 256]
                    dst = w1g[:, slot * NK * 256:(slot + 1) * NK * 256].rearrange(
                        "p (k j) -> p k j", k=NK)
                    v = s_w1[slot].inc(16)
                    P.do("gpsimd", lambda E, d=dst, s=src, h=s_w1[slot].h:
                         E.dma_start(d, s).then_inc(h, 16))
                    return v

                def w2dma(gi, fc=fc, e=e):
                    slot = gi % 2
                    if w2_free[slot] is not None:
                        P.wait("gpsimd", s_pe, w2_free[slot])
                    src = dW2[e].rearrange("(k p) f -> p k f", p=128)[
                        :, fc * 16:(fc + 1) * 16, gi * 256:(gi + 1) * 256]
                    dst = w2g[:, slot * 16 * 256:(slot + 1) * 16 * 256].rearrange(
                        "p (k j) -> p k j", k=16)
                    v = s_w2[slot].inc(16)
                    P.do("gpsimd", lambda E, d=dst, s=src, h=s_w2[slot].h:
                         E.dma_start(d, s).then_inc(h, 16))
                    return v

                w1_v = {gi: w1dma(gi) for gi in range(3)}
                w2_v = {}

                P.wait("scalar", s_sc, bprep_done[e])
                for gi in range(8):
                    slot = gi % 3
                    P.wait("tensor", s_w1[slot], w1_v[gi])
                    for mf2 in range(2):
                        mf = gi * 2 + mf2
                        gmf = fc * 16 + mf
                        for t in range(NT):
                            sl = pool.acquire()
                            def mmp1(E, sl=sl, slot=slot, mf2=mf2, t=t):
                                for k in range(NK):
                                    ins = E.matmul(
                                        sl["ap"],
                                        w1g[:, slot * NK * 256 + k * 256 + mf2 * 128:
                                            slot * NK * 256 + k * 256 + (mf2 + 1) * 128],
                                        Qb[:, k * S + t * TW: k * S + (t + 1) * TW],
                                        start=(k == 0), stop=(k == NK - 1))
                                return ins
                            P.do("tensor", lambda E, f=mmp1: f(E).then_inc(s_pe.h, 1))
                            pv = s_pe.inc()
                            P.wait("scalar", s_pe, pv)
                            P.do("scalar", lambda E, sl=sl, mf=mf, gmf=gmf, t=t,
                                 par_e=par_e: E.activation(
                                hfc[:, mf * S + t * TW: mf * S + (t + 1) * TW], sl["ap"],
                                AF.Gelu, bias=b1c[:, par_e * 32 + gmf: par_e * 32 + gmf + 1],
                                scale=1.0).then_inc(s_sc.h, 1))
                            sv = s_sc.inc()
                            sl["free"] = (s_sc, sv)
                    w1_free[slot] = s_pe.n
                    if gi + 3 < 8:
                        w1_v[gi + 3] = w1dma(gi + 3)
                    elif gi == 5:
                        w2_v[0] = w2dma(0)
                        w2_v[1] = w2dma(1)
                    if rt_hooks_here and gi in rt_hooks_here:
                        rt_hooks_here[gi]()
                v_h = s_sc.n
                if e + 1 < NE:
                    if fc == 0:
                        emit_msbc(e + 1, [0, 1])
                    else:
                        emit_msbc(e + 1, [2, 3])
                        bprep_prev = emit_bprep(e + 1)

                P.wait("vector", s_sc, max(msbc_done[e], bprep_done[e]))

                def p2unit(gi, slot, m2, t):
                    m = gi * 2 + m2
                    sl = pool.acquire()
                    def mmp2(E, sl=sl, slot=slot, m2=m2, t=t):
                        for kc in range(16):
                            ins = E.matmul(
                                sl["ap"],
                                w2g[:, slot * 16 * 256 + kc * 256 + m2 * 128:
                                    slot * 16 * 256 + kc * 256 + (m2 + 1) * 128],
                                hfc[:, kc * S + t * TW: kc * S + (t + 1) * TW],
                                start=(kc == 0), stop=(kc == 15))
                        return ins
                    P.do("tensor", lambda E, f=mmp2: f(E).then_inc(s_pe.h, 1))
                    pv = s_pe.inc()
                    P.wait("vector", s_pe, pv)
                    bcol = (b2c[:, par_e * 8 + m: par_e * 8 + m + 1]
                            if fc == 0 else zero_c[:, :])
                    if e == 0 and fc == 0:
                        P.do("vector", lambda E, sl=sl, m=m, t=t, bcol=bcol,
                             par_e=par_e: E.scalar_tensor_tensor(
                                 Pb[:, m * S + t * TW: m * S + (t + 1) * TW],
                                 sl["ap"], bcol,
                                 ms_bc[:, par_e * S + t * TW: par_e * S + (t + 1) * TW],
                                 ALU.add, ALU.mult).then_inc(s_ve.h, 1))
                    else:
                        P.do("vector", lambda E, sl=sl, t=t, bcol=bcol,
                             par_e=par_e: E.scalar_tensor_tensor(
                                 tmp_bf[:, 0:TW], sl["ap"], bcol,
                                 ms_bc[:, par_e * S + t * TW: par_e * S + (t + 1) * TW],
                                 ALU.add, ALU.mult))
                        P.do("vector", lambda E, m=m, t=t: E.tensor_tensor(
                            Pb[:, m * S + t * TW: m * S + (t + 1) * TW],
                            Pb[:, m * S + t * TW: m * S + (t + 1) * TW],
                            tmp_bf[:, 0:TW],
                            ALU.add).then_inc(s_ve.h, 1))
                    y_ev = s_ve.inc()
                    sl["free"] = (s_ve, y_ev)

                for gi in range(4):
                    slot = gi % 2
                    P.wait("tensor", s_w2[slot], w2_v[gi])
                    P.wait("tensor", s_sc, v_h)
                    if e == NE - 1 and fc == 1 and gi == 3:
                        for t in range(NT):
                            for m2 in range(2):
                                p2unit(gi, slot, m2, t)
                            emit_tail(t)
                    else:
                        for m2 in range(2):
                            for t in range(NT):
                                p2unit(gi, slot, m2, t)
                    w2_free[slot] = s_pe.n
                    if gi + 2 < 4:
                        w2_v[gi + 2] = w2dma(gi + 2)
            v_y_end[e] = s_ve.n
        v_experts = s_ve.n
        tap_dump("tYA", Pb[:, :], s_ve, v_experts)

        P.wait("sync", s_o[0], s_o[0].n)
        P.wait("sync", s_o[1], s_o[1].n)

        with nc.Block() as blk:
            blk.sync(lambda E: [f(E) for f in P.q["sync"]])
            blk.gpsimd(lambda E: [f(E) for f in P.q["gpsimd"]])
            blk.tensor(lambda E: [f(E) for f in P.q["tensor"]])
            blk.scalar(lambda E: [f(E) for f in P.q["scalar"]])
            blk.vector(lambda E: [f(E) for f in P.q["vector"]])

    return nc


def kernel(**inputs):
    xs = np.ascontiguousarray(inputs["L0_attn_x"], dtype=np.float32)
    base = {
        "ln_g": np.ascontiguousarray(inputs["L0_attn_ln_g"], np.float32).reshape(1, D),
        "ln_b": np.ascontiguousarray(inputs["L0_attn_ln_b"], np.float32).reshape(1, D),
        "mix_w": np.ascontiguousarray(inputs["L0_attn_mix_w"], np.float32).reshape(D, 3),
        "mix_b": np.ascontiguousarray(inputs["L0_attn_mix_b"], np.float32).reshape(1, D),
        "Wi": np.ascontiguousarray(inputs["L0_attn_Wi"], np.float32),
        "bi": np.ascontiguousarray(inputs["L0_attn_bi"], np.float32).reshape(1, D),
        "Wsin": np.ascontiguousarray(inputs["L0_attn_Wsin"], np.float32),
        "bsin": np.ascontiguousarray(inputs["L0_attn_bsin"], np.float32).reshape(1, SD),
        "Wd": np.ascontiguousarray(inputs["L0_attn_Wd"], np.float32),
        "bd": np.ascontiguousarray(inputs["L0_attn_bd"], np.float32).reshape(1, SD),
        "Wsel": np.ascontiguousarray(inputs["L0_attn_Wsel"], np.float32),
        "bsel": np.ascontiguousarray(inputs["L0_attn_bsel"], np.float32).reshape(1, SD),
        "Wso": np.ascontiguousarray(inputs["L0_attn_Wso"], np.float32),
        "bso": np.ascontiguousarray(inputs["L0_attn_bso"], np.float32).reshape(1, D),
        "Wg": np.ascontiguousarray(inputs["L0_attn_Wg"], np.float32),
        "bg": np.ascontiguousarray(inputs["L0_attn_bg"], np.float32).reshape(1, D),
        "Wout": np.ascontiguousarray(inputs["L0_attn_Wout"], np.float32),
        "bout": np.ascontiguousarray(inputs["L0_attn_bout"], np.float32).reshape(1, D),
        "w_red": np.ascontiguousarray(inputs["L0_ffn_w_red"], np.float32),
        "wg8": np.ascontiguousarray(inputs["L0_ffn_wg"], np.float32),
        "w1": np.ascontiguousarray(inputs["L0_ffn_w1"], np.float32),
        "b1": np.ascontiguousarray(inputs["L0_ffn_b1"], np.float32),
        "w2": np.ascontiguousarray(inputs["L0_ffn_w2"], np.float32),
        "b2": np.ascontiguousarray(inputs["L0_ffn_b2"], np.float32),
        "ident": np.eye(128, dtype=np.float32),
        "identb": np.eye(8, dtype=np.float32).astype(ml_dtypes.bfloat16),
        "e8": np.repeat(np.eye(8, dtype=np.float32), 128, axis=1).astype(ml_dtypes.bfloat16),
        "ones_f": np.ones((1, 128), np.float32),
        "ones_b": np.ones((1, TW), ml_dtypes.bfloat16),
        "ones_c": np.ones((128, 1), ml_dtypes.bfloat16),
        "zero_c": np.zeros((128, 1), np.float32),
    }
    nc = build(taps=False)
    in_maps = [dict(base, x=np.ascontiguousarray(xs[b])) for b in range(8)]
    res = run_bass_kernel_spmd(nc, in_maps, core_ids=list(range(8)))
    return np.stack([np.asarray(r["y"], np.float32) for r in res.results], axis=0)


# version 13 (final)
# speedup vs baseline: 1.1313x; 1.0048x over previous
import contextlib
import numpy as np
import ml_dtypes
import concourse.bass as bass
import concourse.mybir as mybir
from concourse.bass_utils import run_bass_kernel_spmd

F32 = mybir.dt.float32
BF16 = mybir.dt.bfloat16
AF = mybir.ActivationFunctionType
ALU = mybir.AluOpType
AX = mybir.AxisListType

D = 1024
S = 2048
SD = 256
FF = 4096
NE = 8
NK = D // 128
NT = 4
TW = 512
SP = S + 2
FCW = 2048
NMF = FCW // 128
EPS = 1e-5

ENGS = ("sync", "gpsimd", "tensor", "scalar", "vector")


class Sem:
    def __init__(self, h):
        self.h = h
        self.n = 0

    def inc(self, k=1):
        self.n += k
        return self.n


class Prog:
    def __init__(self):
        self.q = {e: [] for e in ENGS}
        self.lastw = {}

    def do(self, eng, fn):
        self.q[eng].append(fn)

    def wait(self, eng, sem, v):
        if v <= 0:
            return
        key = (eng, id(sem))
        if self.lastw.get(key, -1) >= v:
            return
        self.lastw[key] = v
        self.q[eng].append(lambda E, h=sem.h, v=v: E.wait_ge(h, v))


class Pool:

    def __init__(self, prog, slots):
        self.prog = prog
        self.slots = [{"ap": s, "free": None} for s in slots]
        self.i = 0

    def acquire(self):
        s = self.slots[self.i % len(self.slots)]
        self.i += 1
        if s["free"] is not None:
            self.prog.wait("tensor", s["free"][0], s["free"][1])
        return s


def build(taps=False):
    nc = bass.Bass("TRN2", target_bir_lowering=False)

    dX = nc.dram_tensor("x", [S, D], F32, kind="ExternalInput").ap()
    dLNG = nc.dram_tensor("ln_g", [1, D], F32, kind="ExternalInput").ap()
    dLNB = nc.dram_tensor("ln_b", [1, D], F32, kind="ExternalInput").ap()
    dMIXW = nc.dram_tensor("mix_w", [D, 3], F32, kind="ExternalInput").ap()
    dMIXB = nc.dram_tensor("mix_b", [1, D], F32, kind="ExternalInput").ap()
    dWI = nc.dram_tensor("Wi", [D, D], F32, kind="ExternalInput").ap()
    dBI = nc.dram_tensor("bi", [1, D], F32, kind="ExternalInput").ap()
    dWSIN = nc.dram_tensor("Wsin", [D, SD], F32, kind="ExternalInput").ap()
    dBSIN = nc.dram_tensor("bsin", [1, SD], F32, kind="ExternalInput").ap()
    dWD = nc.dram_tensor("Wd", [D, SD], F32, kind="ExternalInput").ap()
    dBD = nc.dram_tensor("bd", [1, SD], F32, kind="ExternalInput").ap()
    dWSEL = nc.dram_tensor("Wsel", [D, SD], F32, kind="ExternalInput").ap()
    dBSEL = nc.dram_tensor("bsel", [1, SD], F32, kind="ExternalInput").ap()
    dWSO = nc.dram_tensor("Wso", [SD, D], F32, kind="ExternalInput").ap()
    dBSO = nc.dram_tensor("bso", [1, D], F32, kind="ExternalInput").ap()
    dWG = nc.dram_tensor("Wg", [D, D], F32, kind="ExternalInput").ap()
    dBG = nc.dram_tensor("bg", [1, D], F32, kind="ExternalInput").ap()
    dWOUT = nc.dram_tensor("Wout", [D, D], F32, kind="ExternalInput").ap()
    dBOUT = nc.dram_tensor("bout", [1, D], F32, kind="ExternalInput").ap()
    dWRED = nc.dram_tensor("w_red", [D, 16], F32, kind="ExternalInput").ap()
    dWG8 = nc.dram_tensor("wg8", [8, 16], F32, kind="ExternalInput").ap()
    dW1 = nc.dram_tensor("w1", [NE, D, FF], F32, kind="ExternalInput").ap()
    dB1 = nc.dram_tensor("b1", [NE, FF], F32, kind="ExternalInput").ap()
    dW2 = nc.dram_tensor("w2", [NE, FF, D], F32, kind="ExternalInput").ap()
    dB2 = nc.dram_tensor("b2", [NE, D], F32, kind="ExternalInput").ap()
    dIDENT = nc.dram_tensor("ident", [128, 128], F32, kind="ExternalInput").ap()
    dIDENTB = nc.dram_tensor("identb", [8, 8], BF16, kind="ExternalInput").ap()
    dE8 = nc.dram_tensor("e8", [8, 1024], BF16, kind="ExternalInput").ap()
    dONESF = nc.dram_tensor("ones_f", [1, 128], F32, kind="ExternalInput").ap()
    dONESB = nc.dram_tensor("ones_b", [1, TW], BF16, kind="ExternalInput").ap()
    dONESC = nc.dram_tensor("ones_c", [128, 1], BF16, kind="ExternalInput").ap()
    dZEROC = nc.dram_tensor("zero_c", [128, 1], F32, kind="ExternalInput").ap()

    dY = nc.dram_tensor("y", [S, D], F32, kind="ExternalOutput").ap()

    tap_t = {}
    if taps:
        for name, shape in [
            ("tA", [128, NK * S]), ("tH", [128, NK * SP]), ("tMIX", [128, NK * TW]),
            ("tDEC", [128, 2 * TW]), ("tST", [128, 2 * TW]), ("tU", [128, NK * TW]),
            ("tG", [128, NK * TW]), ("tPA", [128, NK * S]), ("tQ", [128, NK * S]),
            ("tMS", [8, S]), ("tYA", [128, NK * S]), ("tSB", [128, NK * S]),
            ("tRED", [16, S]), ("tLG", [8, S]), ("tSCR", [128, 16]),
        ]:
            tap_t[name] = nc.dram_tensor(name, shape, F32, kind="ExternalOutput").ap()

    P = Prog()
    es = contextlib.ExitStack()
    with es:
        def mksb(stack):
            return lambda name, shape, dt: stack.enter_context(nc.sbuf_tensor(name, shape, dt))

        sb = mksb(es)

        Pb = sb("Pb", [128, NK * S], BF16)
        ident = sb("ident_sb", [128, 128], F32)
        identb = sb("identb_sb", [8, 8], BF16)
        e8 = sb("e8_sb", [8, 1024], BF16)
        ones_f = sb("ones_f_sb", [1, 128], F32)
        ones_c = sb("ones_c_sb", [128, 1], BF16)
        zero_c = sb("zero_c_sb", [128, 1], F32)
        mu_r = sb("mu_r", [1, TW], F32)
        e2e_r = sb("e2e_r", [1, TW], F32)
        rstd_r = sb("rstd_r", [1, TW], F32)
        nrm_s = sb("nrm_s", [128, TW], F32)
        sq_bf = sb("sq_bf", [128, 2 * TW], BF16)
        lng_c = sb("lng_c", [128, NK], F32)
        lnb_c = sb("lnb_c", [128, NK], F32)
        mixw_c = sb("mixw_c", [128, NK * 3], F32)
        mixb_c = sb("mixb_c", [128, NK], F32)
        bi_c = sb("bi_c", [128, NK], F32)
        bg_c = sb("bg_c", [128, NK], F32)
        bout_c = sb("bout_c", [128, NK], F32)
        bso_c = sb("bso_c", [128, NK], F32)
        bsin_c = sb("bsin_c", [128, 2], F32)
        bd_c = sb("bd_c", [128, 2], F32)
        bsel_c = sb("bsel_c", [128, 2], F32)
        wg8_s = sb("wg8_s", [8, 16], F32)
        wred_b = sb("wred_b", [128, NK * 16], BF16)

        psA = es.enter_context(nc.psum_tensor("psA", [128, 1024], F32))
        psB = es.enter_context(nc.psum_tensor("psB", [128, 1024], F32))
        psC = es.enter_context(nc.psum_tensor("psC", [128, 1024], F32))
        psD = es.enter_context(nc.psum_tensor("psD", [128, 512], F32))
        psE = es.enter_context(nc.psum_tensor("psE", [128, 512], BF16))

        s_ld = Sem(es.enter_context(nc.semaphore("s_ld")))
        s_ls = Sem(es.enter_context(nc.semaphore("s_ls")))
        s_gl = Sem(es.enter_context(nc.semaphore("s_gl")))
        s_gp = Sem(es.enter_context(nc.semaphore("s_gp")))
        s_x = [Sem(es.enter_context(nc.semaphore(f"s_x{i}"))) for i in range(3)]
        s_w1 = [Sem(es.enter_context(nc.semaphore(f"s_w1{i}"))) for i in range(3)]
        s_w2 = [Sem(es.enter_context(nc.semaphore(f"s_w2{i}"))) for i in range(2)]
        s_b = Sem(es.enter_context(nc.semaphore("s_b")))
        s_o = [Sem(es.enter_context(nc.semaphore(f"s_o{i}"))) for i in range(2)]
        s_pe = Sem(es.enter_context(nc.semaphore("s_pe")))
        s_sc = Sem(es.enter_context(nc.semaphore("s_sc")))
        s_ve = Sem(es.enter_context(nc.semaphore("s_ve")))
        s_st = Sem(es.enter_context(nc.semaphore("s_st")))

        pool = Pool(P, [psA[:, 0:512], psA[:, 512:1024],
                        psB[:, 0:512], psB[:, 512:1024]])

        def sdma(dst, src):
            v = s_ld.inc(16)
            P.do("sync", lambda E, d=dst, s=src, h=s_ld.h: E.dma_start(d, s).then_inc(h, 16))
            return v

        def gdma(dst, src):
            v = s_gl.inc(16)
            P.do("gpsimd", lambda E, d=dst, s=src, h=s_gl.h: E.dma_start(d, s).then_inc(h, 16))
            return v

        def scdma(dst, src):
            v = s_ls.inc(16)
            P.do("scalar", lambda E, d=dst, s=src, h=s_ls.h: E.dma_start(d, s).then_inc(h, 16))
            return v

        def tap_dump(name, src_ap, wait_sem, wait_v):
            if not taps:
                return
            P.wait("gpsimd", wait_sem, wait_v)
            s_st.inc(16)
            P.do("gpsimd", lambda E, d=tap_t[name][:, :], s=src_ap, h=s_st.h:
                 E.dma_start(d, s).then_inc(h, 16))

        v_const = 0
        for dst, src in [(ident, dIDENT), (identb, dIDENTB), (e8, dE8), (ones_f, dONESF),
                         (ones_c, dONESC), (zero_c, dZEROC),
                         (wg8_s, dWG8)]:
            v_const = sdma(dst[:, :], src[:, :])
        v_mixw = sdma(mixw_c[:, :].rearrange("p (c j) -> p c j", c=NK),
                      dMIXW.rearrange("(c p) j -> p c j", p=128))
        v_wred = gdma(wred_b[:, :].rearrange("p (c m) -> p c m", c=NK),
                      dWRED.rearrange("(c p) m -> p c m", p=128))

        es_attn = es.enter_context(contextlib.ExitStack())
        sbA = mksb(es_attn)
        xstage = sbA("xstage", [128, 3 * 1024], F32)
        wi_b = sbA("wi_b", [128, NK * 1024], BF16)
        wg_b = sbA("wg_b", [128, NK * 1024], BF16)
        wout_b = sbA("wout_b", [128, NK * 1024], BF16)
        wsin_b = sbA("wsin_b", [128, NK * SD], BF16)
        wd_b = sbA("wd_b", [128, NK * SD], BF16)
        wsel_b = sbA("wsel_b", [128, NK * SD], BF16)
        wso_b = sbA("wso_b", [128, 2 * 1024], BF16)
        h_pad = sbA("h_pad", [128, NK * SP], BF16)
        mixed_bf = sbA("mixed_bf", [128, 2 * NK * TW], BF16)
        u_bf = sbA("u_bf", [128, NK * TW], BF16)
        g_bf = sbA("g_bf", [128, NK * TW], BF16)
        dec_f = sbA("dec_f", [128, 2 * TW], F32)
        xst_f = sbA("xst_f", [128, 2 * TW], F32)
        sel_f = sbA("sel_f", [128, 2 * TW], BF16)
        st_f = sbA("st_f", [128, 2 * TW], F32)
        ss_bf = sbA("ss_bf", [128, 2 * TW], BF16)
        carry = sbA("carry", [128, 2], F32)
        s0_bf = sbA("s0_bf", [128, TW], BF16)
        ctmp = sbA("ctmp", [128, TW], BF16)

        v_w = 0
        for dst, src, w in [(wsin_b, dWSIN, SD), (wd_b, dWD, SD), (wsel_b, dWSEL, SD)]:
            v_w = gdma(dst[:, :].rearrange("p (c m) -> p c m", c=NK),
                       src.rearrange("(c p) m -> p c m", p=128))
        v_w = gdma(wso_b[:, :].rearrange("p (c m) -> p c m", c=2),
                   dWSO.rearrange("(c p) m -> p c m", p=128))
        for dst, src in [(wi_b, dWI), (wg_b, dWG), (wout_b, dWOUT)]:
            v_w = gdma(dst[:, :].rearrange("p (c m) -> p c m", c=NK),
                       src.rearrange("(c p) m -> p c m", p=128))
        v_attn_w = v_w

        P.wait("tensor", s_ld, v_const)

        def row_to_cols(row_of, n, dst):
            def t_ops(E, srcs=[row_of(i) for i in range(n)]):
                for i, src in enumerate(srcs):
                    ins = E.transpose(psD[:, i:i + 1], src, ident[0:1, 0:1])
                return ins
            P.do("tensor", lambda E, f=t_ops: f(E).then_inc(s_pe.h, 1))
            pv = s_pe.inc()
            P.wait("scalar", s_pe, pv)
            P.do("scalar", lambda E, d=dst, n=n: E.copy(d, psD[:, 0:n]).then_inc(s_sc.h, 1))
            return s_sc.inc()

        row_waves = [
            [(dLNG, lng_c, NK), (dLNB, lnb_c, NK), (dMIXB, mixb_c, NK)],
            [(dBI, bi_c, NK), (dBG, bg_c, NK), (dBOUT, bout_c, NK)],
            [(dBSO, bso_c, NK), (dBSIN, bsin_c, 2), (dBD, bd_c, 2),
             (dBSEL, bsel_c, 2)],
        ]
        prev_cols = 0
        wave_pe = 0
        for wave in row_waves:
            P.wait("sync", s_pe, wave_pe)
            off = 0
            offs = []
            v_wave = 0
            for src, dst, n in wave:
                offs.append(off)
                v_wave = sdma(xstage[0:1, off:off + n * 128], src[0:1, :])
                off += n * 128
            P.wait("tensor", s_ld, v_wave)
            for (src, dst, n), o in zip(wave, offs):
                P.wait("tensor", s_sc, prev_cols)
                prev_cols = row_to_cols(
                    lambda j, o=o: xstage[0:1, o + j * 128:o + (j + 1) * 128],
                    n, dst[:, 0:n])
            wave_pe = s_pe.n
        v_cols = prev_cols
        v_bias_cols = prev_cols

        sqslot_free = {}
        ln_state = {"prev": 0}

        def emit_ln(src_ap, dst_ap, gain, out_chunks=NK, tiles=None):
            if tiles is None:
                tiles = range(NT)
            ln_done = 0
            for t in tiles:
                P.wait("tensor", s_ve, ln_state["prev"])
                pv = 0
                for k in range(NK):
                    slot = k % 2
                    if slot in sqslot_free:
                        P.wait("scalar", s_pe, sqslot_free[slot])
                    P.do("scalar", lambda E, k=k, t=t, slot=slot: E.activation(
                        sq_bf[:, slot * TW:(slot + 1) * TW], src_ap(k, t),
                        AF.Square).then_inc(s_sc.h, 1))
                    sv = s_sc.inc()
                    P.do("tensor", lambda E, k=k, t=t: E.matmul(
                        psC[0:1, 0:TW], ones_c[:, :], src_ap(k, t),
                        start=(k == 0), stop=(k == NK - 1)))
                    P.wait("tensor", s_sc, sv)
                    P.do("tensor", lambda E, k=k, slot=slot: E.matmul(
                        psC[0:1, 512:512 + TW], ones_c[:, :],
                        sq_bf[:, slot * TW:(slot + 1) * TW],
                        start=(k == 0), stop=(k == NK - 1)).then_inc(s_pe.h, 1))
                    pv = s_pe.inc()
                    sqslot_free[slot] = pv
                v_stats = pv
                P.wait("vector", s_pe, v_stats)
                P.do("vector", lambda E: E.tensor_scalar(mu_r[:, :], psC[0:1, 0:TW],
                                                         1.0 / D, None, ALU.mult))
                P.do("vector", lambda E: E.tensor_scalar(e2e_r[:, :], psC[0:1, 512:512 + TW],
                                                         1.0 / D, EPS, ALU.mult, ALU.add))
                P.do("vector", lambda E: E.tensor_tensor(rstd_r[:, :], mu_r[:, :], mu_r[:, :],
                                                         ALU.mult))
                P.do("vector", lambda E: E.tensor_tensor(e2e_r[:, :], e2e_r[:, :], rstd_r[:, :],
                                                         ALU.subtract).then_inc(s_ve.h, 1))
                vv = s_ve.inc()
                P.wait("scalar", s_ve, vv)
                P.do("scalar", lambda E: E.activation(rstd_r[:, :], e2e_r[:, :],
                                                      AF.Ln).then_inc(s_sc.h, 1))
                P.wait("scalar", s_sc, s_sc.inc())
                P.do("scalar", lambda E: E.activation(e2e_r[:, :], rstd_r[:, :],
                                                      AF.Exp, scale=-0.5).then_inc(s_sc.h, 1))
                sv2 = s_sc.inc()
                P.wait("vector", s_sc, sv2)
                P.do("vector", lambda E: E.tensor_tensor(mu_r[:, :], mu_r[:, :], e2e_r[:, :],
                                                         ALU.mult).then_inc(s_ve.h, 1))
                vv2 = s_ve.inc()
                P.wait("tensor", s_ve, vv2)
                P.do("tensor", lambda E: E.matmul(psC[:, 0:TW], ones_f[0:1, 0:128],
                                                  e2e_r[:, :], start=True, stop=True))
                P.do("tensor", lambda E: E.matmul(psC[:, 512:512 + TW], ones_f[0:1, 0:128],
                                                  mu_r[:, :], start=True,
                                                  stop=True).then_inc(s_pe.h, 1))
                v_bc = s_pe.inc()
                P.wait("vector", s_pe, v_bc)
                for k in range(out_chunks):
                    P.do("vector", lambda E, k=k, t=t: E.tensor_tensor(
                        nrm_s[:, :], src_ap(k, t), psC[:, 0:TW], ALU.mult))
                    if gain:
                        P.do("vector", lambda E, k=k, t=t: E.tensor_tensor(
                            nrm_s[:, :], nrm_s[:, :], psC[:, 512:512 + TW], ALU.subtract))
                        P.do("vector", lambda E, k=k, t=t: E.tensor_scalar(
                            dst_ap(k, t), nrm_s[:, :], lng_c[:, k:k + 1], lnb_c[:, k:k + 1],
                            ALU.mult, ALU.add))
                    else:
                        P.do("vector", lambda E, k=k, t=t: E.tensor_tensor(
                            dst_ap(k, t), nrm_s[:, :], psC[:, 512:512 + TW], ALU.subtract))
                P.do("vector", lambda E: E.engine_nop().then_inc(s_ve.h, 1))
                ln_done = s_ve.inc()
                ln_state["prev"] = ln_done
            return ln_done

        A_src = lambda k, t: Pb[:, k * S + t * TW: k * S + (t + 1) * TW]
        h_dst = lambda k, t: h_pad[:, k * SP + 1 + t * TW: k * SP + 1 + (t + 1) * TW]

        for k in range(NK):
            P.do("vector", lambda E, k=k: E.memset(h_pad[:, k * SP:k * SP + 1], 0.0))
            P.do("vector", lambda E, k=k: E.memset(
                h_pad[:, k * SP + 1 + S:k * SP + 2 + S], 0.0))
        P.do("vector", lambda E: E.engine_nop().then_inc(s_ve.h, 1))
        s_ve.inc()
        P.do("scalar", lambda E: E.memzero(carry[:, :]).then_inc(s_sc.h, 1))
        v_carry = s_sc.inc()

        MW = NK * TW
        mix_rhs = lambda k, par: mixed_bf[:, par * MW + k * TW: par * MW + (k + 1) * TW]
        cpe_hist = [0, 0]
        conv_done = {}

        def emit_conv(t):
            par = t % 2
            if t >= 2:
                P.wait("vector", s_pe, cpe_hist[par])
            for k in range(NK):
                base = k * SP + t * TW
                P.do("vector", lambda E, k=k, base=base: E.tensor_scalar(
                    s0_bf[:, 0:TW], h_pad[:, base: base + TW],
                    mixw_c[:, k * 3:k * 3 + 1], mixb_c[:, k:k + 1],
                    ALU.mult, ALU.add))
                P.do("vector", lambda E, k=k, base=base: E.scalar_tensor_tensor(
                    ctmp[:, 0:TW], h_pad[:, base + 1: base + 1 + TW],
                    mixw_c[:, k * 3 + 1:k * 3 + 2], s0_bf[:, 0:TW],
                    ALU.mult, ALU.add))
                def g3(E, k=k, base=base, par=par):
                    return E.scalar_tensor_tensor(
                        mix_rhs(k, par), h_pad[:, base + 2: base + 2 + TW],
                        mixw_c[:, k * 3 + 2:k * 3 + 3], ctmp[:, 0:TW],
                        ALU.mult, ALU.add)
                P.do("vector", lambda E, f=g3: f(E).then_inc(s_ve.h, 1))
                s_ve.inc()
            conv_done[t] = s_ve.n

        P.wait("sync", s_pe, wave_pe)
        P.wait("scalar", s_pe, wave_pe)
        xs_free = [None, None, None]
        ln1_marks = {}
        for g in range(16):
            slot = g % 3
            q = "sync" if g % 2 == 0 else "scalar"
            if xs_free[slot] is not None:
                P.wait(q, xs_free[slot][0], xs_free[slot][1])
            dstx = xstage[:, slot * 1024:(slot + 1) * 1024]
            srcx = dX[g * 128:(g + 1) * 128, :]
            vx = s_x[slot].inc(16)
            P.do(q, lambda E, d=dstx, s=srcx, h=s_x[slot].h:
                 E.dma_start(d, s).then_inc(h, 16))
            P.wait("tensor", s_x[slot], vx)
            for half in range(2):
                sl = pool.acquire()
                def t_tr(E, sl=sl, slot=slot, half=half):
                    for j in range(4):
                        k = half * 4 + j
                        ins = E.transpose(sl["ap"][:, j * 128:(j + 1) * 128],
                                          xstage[:, slot * 1024 + k * 128:
                                                 slot * 1024 + (k + 1) * 128],
                                          ident[:, :])
                    return ins
                P.do("tensor", lambda E, f=t_tr: f(E).then_inc(s_pe.h, 1))
                pv = s_pe.inc()
                P.wait("scalar", s_pe, pv)
                out_ap = Pb[:, :].rearrange("p (c s) -> p c s", c=NK)[
                    :, half * 4:(half + 1) * 4, g * 128:(g + 1) * 128]
                P.do("scalar", lambda E, o=out_ap, sl=sl: E.copy(
                    o, sl["ap"][:, 0:512].rearrange("p (c s) -> p c s", c=4)).then_inc(s_sc.h, 1))
                sv = s_sc.inc()
                sl["free"] = (s_sc, sv)
                if half == 1:
                    xs_free[slot] = (s_sc, sv)
            if g % 4 == 3:
                t_ln = g // 4
                xmark = s_sc.n
                P.wait("tensor", s_sc, xmark)
                P.wait("vector", s_sc, xmark)
                ln1_marks[t_ln] = emit_ln(A_src, h_dst, gain=True, tiles=[t_ln])
                if t_ln == 1:
                    emit_conv(0)
        v_A_done = s_sc.n
        v_ln1 = ln1_marks[NT - 1]
        tap_dump("tA", Pb[:, :], s_sc, v_A_done)
        tap_dump("tH", h_pad[:, :], s_ve, v_ln1)

        P.wait("tensor", s_gl, v_attn_w)
        P.wait("scalar", s_ve, v_ln1)
        prev = {"dxs_ve": 0, "sel_ve": 0, "g_ve": 0}
        ss_rhs = lambda k: ss_bf[:, k * TW:(k + 1) * TW]

        def emit_proj(wbuf, wwidth, bcol, n_k, m, act, dst_ap, v_rhs, par):
            sl = pool.acquire()
            P.wait("tensor", s_ve, v_rhs)
            def mmf(E, sl=sl, wbuf=wbuf, wwidth=wwidth, n_k=n_k, m=m, par=par):
                for k in range(n_k):
                    ins = E.matmul(sl["ap"],
                                   wbuf[:, k * wwidth + m * 128: k * wwidth + (m + 1) * 128],
                                   mix_rhs(k, par), start=(k == 0), stop=(k == n_k - 1))
                return ins
            P.do("tensor", lambda E, f=mmf: f(E).then_inc(s_pe.h, 1))
            pv = s_pe.inc()
            P.wait("scalar", s_pe, pv)
            P.do("scalar", lambda E, sl=sl, act=act, dst=dst_ap, bcol=bcol: E.activation(
                dst, sl["ap"], act, bias=bcol, scale=1.0).then_inc(s_sc.h, 1))
            sv = s_sc.inc()
            sl["free"] = (s_sc, sv)
            return sv

        for t in range(NT):
            par = t % 2
            v_mix = conv_done[t]

            P.wait("scalar", s_ve, prev["dxs_ve"])
            v_xst = [emit_proj(wsin_b, SD, bsin_c[:, m:m + 1], NK, m, AF.Tanh,
                               xst_f[:, m * TW:(m + 1) * TW], v_mix, par) for m in range(2)]
            v_dec = [emit_proj(wd_b, SD, bd_c[:, m:m + 1], NK, m, AF.Sigmoid,
                               dec_f[:, m * TW:(m + 1) * TW], v_mix, par) for m in range(2)]
            P.wait("vector", s_sc, v_dec[1])
            P.wait("vector", s_sc, v_carry)
            for c in range(2):
                P.do("vector", lambda E, c=c: E.tensor_tensor(
                    nrm_s[:, :], dec_f[:, c * TW:(c + 1) * TW],
                    xst_f[:, c * TW:(c + 1) * TW], ALU.mult))
                P.do("vector", lambda E, c=c: E.tensor_tensor(
                    xst_f[:, c * TW:(c + 1) * TW], xst_f[:, c * TW:(c + 1) * TW],
                    nrm_s[:, :], ALU.subtract))
                P.do("vector", lambda E, c=c: E.tensor_tensor_scan(
                    st_f[:, c * TW:(c + 1) * TW], dec_f[:, c * TW:(c + 1) * TW],
                    xst_f[:, c * TW:(c + 1) * TW], carry[:, c:c + 1], ALU.mult, ALU.add))
            P.do("vector", lambda E: E.engine_nop().then_inc(s_ve.h, 1))
            v_scan = s_ve.inc()
            prev["dxs_ve"] = v_scan
            P.wait("scalar", s_ve, v_scan)
            P.do("scalar", lambda E: E.copy(carry[:, 0:1], st_f[:, TW - 1: TW]))
            P.do("scalar", lambda E: E.copy(carry[:, 1:2],
                                            st_f[:, 2 * TW - 1: 2 * TW]).then_inc(s_sc.h, 1))
            v_carry = s_sc.inc()
            if t + 1 < NT:
                emit_conv(t + 1)

            P.wait("scalar", s_ve, prev["g_ve"])
            v_u = [emit_proj(wi_b, 1024, bi_c[:, m:m + 1], NK, m, AF.Tanh,
                             u_bf[:, m * TW:(m + 1) * TW], v_mix, par) for m in range(NK)]
            P.wait("scalar", s_ve, prev["sel_ve"])
            v_sel = [emit_proj(wsel_b, SD, bsel_c[:, m:m + 1], NK, m, AF.Sigmoid,
                               sel_f[:, m * TW:(m + 1) * TW], v_mix, par) for m in range(2)]
            v_g = [emit_proj(wg_b, 1024, bg_c[:, m:m + 1], NK, m, AF.Sigmoid,
                             g_bf[:, m * TW:(m + 1) * TW], v_mix, par) for m in range(NK)]
            v_mix_consumed = s_pe.n

            if t == 0:
                tap_dump("tMIX", mixed_bf[:, 0:MW], s_pe, v_mix_consumed)
                tap_dump("tDEC", dec_f[:, :], s_ve, v_scan)
                tap_dump("tST", st_f[:, :], s_ve, v_scan)

            P.wait("vector", s_sc, v_g[NK - 1])
            for k in range(NK):
                P.do("vector", lambda E, k=k: E.tensor_tensor(
                    ctmp[:, 0:TW], u_bf[:, k * TW:(k + 1) * TW],
                    g_bf[:, k * TW:(k + 1) * TW], ALU.mult))
                P.do("vector", lambda E, k=k: E.tensor_tensor(
                    u_bf[:, k * TW:(k + 1) * TW], u_bf[:, k * TW:(k + 1) * TW],
                    ctmp[:, 0:TW], ALU.subtract))
            P.do("vector", lambda E: E.tensor_tensor(ss_bf[:, :], sel_f[:, :], st_f[:, :],
                                                     ALU.mult))
            P.do("vector", lambda E: E.engine_nop().then_inc(s_ve.h, 1))
            v_ss = s_ve.inc()
            prev["sel_ve"] = v_ss

            if t == 0:
                tap_dump("tU", u_bf[:, :], s_ve, v_ss)
                tap_dump("tG", g_bf[:, :], s_sc, v_g[NK - 1])

            v_c_ve = 0
            for m in range(NK):
                sl = pool.acquire()
                P.wait("tensor", s_ve, v_ss)
                def mmso(E, sl=sl, m=m):
                    for kc in range(2):
                        ins = E.matmul(sl["ap"],
                                       wso_b[:, kc * 1024 + m * 128: kc * 1024 + (m + 1) * 128],
                                       ss_rhs(kc), start=(kc == 0), stop=(kc == 1))
                    return ins
                P.do("tensor", lambda E, f=mmso: f(E).then_inc(s_pe.h, 1))
                pv = s_pe.inc()
                P.wait("vector", s_pe, max(pv, v_mix_consumed))
                P.do("vector", lambda E, sl=sl, m=m: E.scalar_tensor_tensor(
                    ctmp[:, 0:TW], sl["ap"],
                    bso_c[:, m:m + 1], g_bf[:, m * TW:(m + 1) * TW],
                    ALU.add, ALU.mult))
                P.do("vector", lambda E, sl=sl, m=m, par=par: E.tensor_tensor(
                    mix_rhs(m, par), ctmp[:, 0:TW],
                    u_bf[:, m * TW:(m + 1) * TW], ALU.add).then_inc(s_ve.h, 1))
                v_c_ve = s_ve.inc()
                sl["free"] = (s_ve, v_c_ve)
            prev["g_ve"] = v_c_ve

            for m in range(NK):
                sl = pool.acquire()
                P.wait("tensor", s_ve, v_c_ve)
                def mmout(E, sl=sl, m=m, par=par):
                    for k in range(NK):
                        ins = E.matmul(sl["ap"],
                                       wout_b[:, k * 1024 + m * 128: k * 1024 + (m + 1) * 128],
                                       mix_rhs(k, par), start=(k == 0), stop=(k == NK - 1))
                    return ins
                P.do("tensor", lambda E, f=mmout: f(E).then_inc(s_pe.h, 1))
                pv = s_pe.inc()
                P.wait("vector", s_pe, pv)
                P.do("vector", lambda E, sl=sl, m=m, t=t: E.scalar_tensor_tensor(
                    A_src(m, t), A_src(m, t), 2.0, sl["ap"], ALU.mult, ALU.add))
                P.do("vector", lambda E, m=m, t=t: E.tensor_scalar(
                    A_src(m, t), A_src(m, t), bout_c[:, m:m + 1], None,
                    ALU.add).then_inc(s_ve.h, 1))
                v_res = s_ve.inc()
                sl["free"] = (s_ve, v_res)
            cpe_hist[par] = s_pe.n
        v_attn_done = s_ve.n
        v_attn_pe = s_pe.n
        tap_dump("tPA", Pb[:, :], s_ve, v_attn_done)

        es_attn.close()
        es_post = es.enter_context(contextlib.ExitStack())
        sbP = mksb(es_post)
        Qb = sbP("Qb", [128, NK * S], BF16)
        hfc = sbP("hfc", [128, NMF * S], BF16)
        ms_all = sbP("ms_all", [8, S], BF16)
        w1g = sbP("w1g", [128, 3 * NK * 256], BF16)
        w2g = sbP("w2g", [128, 2 * 16 * 256], BF16)
        ms_bc = sbP("ms_bc", [128, 2 * S], BF16)
        tmp_bf = sbP("tmp_bf", [128, TW], BF16)
        b1r_f = sbP("b1r_f", [1, FCW], BF16)
        b1c = sbP("b1c", [128, 64], F32)
        b2c = sbP("b2c", [128, 16], F32)

        Q_dst = lambda k, t: Qb[:, k * S + t * TW: k * S + (t + 1) * TW]
        P.wait("tensor", s_ve, v_attn_done)
        P.wait("scalar", s_ve, v_attn_done)
        v_ln2 = emit_ln(A_src, Q_dst, gain=False)
        tap_dump("tQ", Qb[:, :], s_ve, v_ln2)

        P.wait("gpsimd", s_ve, v_ln2)
        P.wait("gpsimd", s_pe, v_attn_pe)
        P.wait("gpsimd", s_sc, s_sc.n)
        w1_free = [None, None, None]
        w2_free = [None, None]

        def bdma(dst, srcap):
            v = s_b.inc(16)
            P.do("gpsimd", lambda E, d=dst, s=srcap, h=s_b.h:
                 E.dma_start(d, s).then_inc(h, 16))
            return v
        v_b1h0 = {0: bdma(b1r_f[:, :], dB1[0:1, 0:FCW])}
        v_b1h1 = {}
        v_b2r = {}

        msbc_half_prev = {0: 0, 1: 0}
        rt = {}
        msbc_done = {}
        bprep_done = {}
        bprep_prev = 0
        v_y_end = {}

        def emit_msbc(e, ts):
            par = e % 2
            if e >= 2:
                P.wait("scalar", s_ve, v_y_end[e - 2])
            for t in ts:
                half = t % 2
                P.wait("tensor", s_sc, msbc_half_prev[half])
                P.do("tensor", lambda E, e=e, t=t, half=half: E.matmul(
                    psC[:, half * 512: half * 512 + TW], e8[0:8, e * 128:(e + 1) * 128],
                    ms_all[:, t * TW:(t + 1) * TW], start=True,
                    stop=True).then_inc(s_pe.h, 1))
                pv = s_pe.inc()
                P.wait("scalar", s_pe, pv)
                P.do("scalar", lambda E, par=par, t=t, half=half: E.copy(
                    ms_bc[:, par * S + t * TW: par * S + (t + 1) * TW],
                    psC[:, half * 512: half * 512 + TW]).then_inc(s_sc.h, 1))
                msbc_half_prev[half] = s_sc.inc()
            if ts[-1] == NT - 1:
                msbc_done[e] = s_sc.n

        def emit_bprep(e):
            par = e % 2
            global_b = bprep_prev
            P.wait("tensor", s_b, v_b1h0[e])
            P.wait("tensor", s_sc, global_b)
            if e >= 2:
                P.wait("scalar", s_ve, v_y_end[e - 2])
            P.wait("tensor", s_sc, rt.get("lgtm", 0))
            def mmtr_b1(E, off):
                for i in range(16):
                    c = 2 * (off + i)
                    ins = E.transpose(psE[:, c:c + 1],
                                      b1r_f[0:1, i * 128:(i + 1) * 128],
                                      identb[0:1, 0:1])
                return ins
            P.do("tensor", lambda E: mmtr_b1(E, 0).then_inc(s_pe.h, 1))
            pv1 = s_pe.inc()
            P.wait("gpsimd", s_pe, pv1)
            v_b1h1[e] = bdma(b1r_f[:, :], dB1[e:e + 1, FCW:FF])
            P.wait("tensor", s_b, v_b1h1[e])
            P.do("tensor", lambda E: mmtr_b1(E, 16).then_inc(s_pe.h, 1))
            pv2 = s_pe.inc()
            P.wait("gpsimd", s_pe, pv2)
            v_b2r[e] = bdma(b1r_f[0:1, 0:D], dB2[e:e + 1, :])
            P.wait("tensor", s_b, v_b2r[e])
            def mmtr_b2(E):
                for i in range(8):
                    ins = E.transpose(psE[:, 64 + 2 * i:65 + 2 * i],
                                      b1r_f[0:1, i * 128:(i + 1) * 128],
                                      identb[0:1, 0:1])
                return ins
            P.do("tensor", lambda E, f=mmtr_b2: f(E).then_inc(s_pe.h, 1))
            pv_b = s_pe.inc()
            P.wait("scalar", s_pe, pv_b)
            ev2 = psE[:, :].rearrange("p (i two) -> p i two", two=2)[:, :, 0]
            P.do("scalar", lambda E, par=par, ev2=ev2: E.copy(
                b1c[:, par * 32:(par + 1) * 32], ev2[:, 0:32]))
            P.do("scalar", lambda E, par=par, ev2=ev2: E.copy(
                b2c[:, par * 8:(par + 1) * 8], ev2[:, 32:40]).then_inc(s_sc.h, 1))
            bprep_done[e] = s_sc.inc()
            if e + 1 < NE:
                P.wait("gpsimd", s_pe, pv_b)
                v_b1h0[e + 1] = bdma(b1r_f[:, :], dB1[e + 1:e + 2, 0:FCW])
            return bprep_done[e]


        es_rt = es.enter_context(contextlib.ExitStack())
        sbR = mksb(es_rt)
        red_f = sbR("red_f", [16, S], BF16)
        lg_f = sbR("lg_f", [8, S], BF16)
        lgtm = sbR("lgtm", [128, 128], F32)
        mstm = sbR("mstm", [128, 128], F32)
        mx_c = sbR("mx_c", [128, 16], F32)
        ssum = sbR("ssum", [128, 16], F32)
        scrs = sbR("scrs", [128, 16], F32)
        exp_s = sbR("exp_s", [128, 8], F32)
        lgs_s = sbR("lgs_s", [128, 16], F32)
        wgT_f = sbR("wgT_f", [16, 8], BF16)
        nrm8 = sbR("nrm8", [8, 2], F32)
        rn8 = sbR("rn8", [8, 1], F32)
        wgsq = sbR("wgsq", [8, 16], F32)

        P.wait("tensor", s_gl, v_wred)
        P.wait("tensor", s_ve, v_ln2)
        red_ev = {}
        for t in range(NT):
            half = t % 2
            if t >= 2:
                P.wait("tensor", s_sc, red_ev[t - 2])
            def mmred(E, t=t, half=half):
                for k in range(NK):
                    ins = E.matmul(psC[0:16, half * 512: half * 512 + TW],
                                   wred_b[:, k * 16:(k + 1) * 16],
                                   Qb[:, k * S + t * TW: k * S + (t + 1) * TW],
                                   start=(k == 0), stop=(k == NK - 1))
                return ins
            P.do("tensor", lambda E, f=mmred: f(E).then_inc(s_pe.h, 1))
            pv = s_pe.inc()
            P.wait("scalar", s_pe, pv)
            P.do("scalar", lambda E, t=t, half=half: E.copy(
                red_f[:, t * TW:(t + 1) * TW],
                psC[0:16, half * 512: half * 512 + TW]).then_inc(s_sc.h, 1))
            red_ev[t] = s_sc.inc()
        v_red = red_ev[NT - 1]
        tap_dump("tRED", red_f[:, :], s_sc, v_red)

        P.wait("scalar", s_ld, v_const)
        P.do("scalar", lambda E: E.activation(wgsq[:, :], wg8_s[:, :],
                                              AF.Square).then_inc(s_sc.h, 1))
        v1 = s_sc.inc()
        P.wait("vector", s_sc, v1)
        P.do("vector", lambda E: E.reduce_sum(nrm8[:, 0:1], wgsq[:, :],
                                              axis=AX.X).then_inc(s_ve.h, 1))
        v2 = s_ve.inc()
        P.wait("scalar", s_ve, v2)
        P.do("scalar", lambda E: E.activation(nrm8[:, 1:2], nrm8[:, 0:1],
                                              AF.Sqrt).then_inc(s_sc.h, 1))
        v3 = s_sc.inc()
        P.wait("vector", s_sc, v3)
        P.do("vector", lambda E: E.tensor_single_scalar(nrm8[:, 1:2], nrm8[:, 1:2], 1e-4,
                                                        ALU.max).then_inc(s_ve.h, 1))
        v4 = s_ve.inc()
        P.wait("scalar", s_ve, v4)
        P.do("scalar", lambda E: E.activation(rn8[:, :], nrm8[:, 1:2],
                                              AF.Ln).then_inc(s_sc.h, 1))
        v4b = s_sc.inc()
        P.wait("vector", s_sc, v4b)
        P.do("vector", lambda E: E.tensor_scalar(rn8[:, :], rn8[:, :], -1.0, None,
                                                 ALU.mult).then_inc(s_ve.h, 1))
        v4c = s_ve.inc()
        P.wait("scalar", s_ve, v4c)
        P.do("scalar", lambda E: E.activation(rn8[:, :], rn8[:, :],
                                              AF.Exp).then_inc(s_sc.h, 1))
        v_rn8 = s_sc.inc()
        P.do("tensor", lambda E: E.transpose(psD[0:16, 0:8], wg8_s[:, :],
                                             ident[0:8, 0:8]).then_inc(s_pe.h, 1))
        v_wgT = s_pe.inc()
        P.wait("scalar", s_pe, v_wgT)
        P.do("scalar", lambda E: E.copy(wgT_f[:, :], psD[0:16, 0:8]).then_inc(s_sc.h, 1))
        v_wgTc = s_sc.inc()

        bprep_prev = v_wgTc
        bprep_prev = emit_bprep(0)

        P.wait("tensor", s_sc, bprep_done[0])
        P.wait("tensor", s_sc, v_red)
        P.wait("scalar", s_sc, v_rn8)
        lg_ev = {}
        for t in range(NT):
            if t >= 1:
                P.wait("tensor", s_ve, lg_ev[t - 1])
            P.do("tensor", lambda E, t=t: E.matmul(psD[0:8, 0:TW], wgT_f[:, :],
                                                   red_f[:, t * TW:(t + 1) * TW],
                                                   start=True, stop=True).then_inc(s_pe.h, 1))
            pv = s_pe.inc()
            P.wait("vector", s_pe, pv)
            P.wait("vector", s_sc, v_rn8)
            P.do("vector", lambda E, t=t: E.tensor_scalar(lg_f[:, t * TW:(t + 1) * TW],
                                                          psD[0:8, 0:TW], rn8[:, :], None,
                                                          ALU.mult).then_inc(s_ve.h, 1))
            lg_ev[t] = s_ve.inc()
        v_lg = lg_ev[NT - 1]
        tap_dump("tLG", lg_f[:, :], s_ve, v_lg)

        msbc_half_prev[0] = red_ev[2]
        msbc_half_prev[1] = red_ev[3]
        mx2 = sbR("mx2", [128, 16], F32)
        scr2 = sbR("scr2", [128, 16], F32)
        exp_done = {}
        ms_ev = {}

        def rt_lgT():
            P.wait("tensor", s_ve, v_lg)
            P.wait("tensor", s_sc, bprep_done[0])
            def mmtr_lg(E):
                for i in range(16):
                    ins = E.transpose(psE[:, i * 8:(i + 1) * 8],
                                      lg_f[0:8, i * 128:(i + 1) * 128], identb[0:8, 0:8])
                return ins
            P.do("tensor", lambda E: mmtr_lg(E).then_inc(s_pe.h, 1))
            v_lgT = s_pe.inc()
            P.wait("scalar", s_pe, v_lgT)
            P.do("scalar", lambda E: E.copy(lgtm[:, :], psE[:, 0:128]).then_inc(s_sc.h, 1))
            rt["lgtm"] = s_sc.inc()
            P.wait("vector", s_sc, rt["lgtm"])
            for i in range(16):
                P.do("vector", lambda E, i=i: E.reduce_max(
                    mx_c[:, i:i + 1], lgtm[:, i * 8:(i + 1) * 8], axis=AX.X))
            P.do("vector", lambda E: E.engine_nop().then_inc(s_ve.h, 1))
            v_mx = s_ve.inc()
            P.wait("scalar", s_ve, v_mx)
            P.do("scalar", lambda E: E.copy(mx2[:, :], mx_c[:, :]).then_inc(s_sc.h, 1))
            rt["mx2"] = s_sc.inc()

        def rt_exp(i0, i1):
            if i0 == 0:
                P.wait("vector", s_sc, rt["mx2"])
            for i in range(i0, i1):
                slot = i % 2
                if i >= 2:
                    P.wait("vector", s_sc, exp_done[i - 2])
                P.do("vector", lambda E, i=i, slot=slot: E.tensor_scalar(
                    lgs_s[:, slot * 8:(slot + 1) * 8], lgtm[:, i * 8:(i + 1) * 8],
                    mx2[:, i:i + 1], None, ALU.subtract).then_inc(s_ve.h, 1))
                vv = s_ve.inc()
                P.wait("scalar", s_ve, vv)
                P.do("scalar", lambda E, i=i, slot=slot: E.activation(
                    exp_s[:, :], lgs_s[:, slot * 8:(slot + 1) * 8], AF.Exp,
                    accum_out=ssum[:, i:i + 1]).then_inc(s_sc.h, 1))
                exp_done[i] = s_sc.inc()

        def rt_fin():
            v_esum = s_sc.n
            P.wait("vector", s_sc, v_esum)
            P.do("vector", lambda E: E.reciprocal(scrs[:, :], ssum[:, :]).then_inc(s_ve.h, 1))
            v_scr = s_ve.inc()
            P.wait("scalar", s_ve, v_scr)
            P.do("scalar", lambda E: E.copy(scr2[:, :], scrs[:, :]).then_inc(s_sc.h, 1))
            v_scr2 = s_sc.inc()
            P.wait("vector", s_sc, v_scr2)
            for i in range(16):
                P.do("vector", lambda E, i=i: E.tensor_scalar(
                    mstm[:, i * 8:(i + 1) * 8], lgtm[:, i * 8:(i + 1) * 8],
                    mx2[:, i:i + 1], None, ALU.is_equal))
                P.do("vector", lambda E, i=i: E.tensor_scalar(
                    mstm[:, i * 8:(i + 1) * 8], mstm[:, i * 8:(i + 1) * 8],
                    scr2[:, i:i + 1], None, ALU.mult))
            P.do("vector", lambda E: E.engine_nop().then_inc(s_ve.h, 1))
            rt["mstm"] = s_ve.inc()

        def rt_mstrans():
            P.wait("tensor", s_ve, rt["mstm"])
            for grp in range(4):
                if grp >= 1:
                    P.wait("tensor", s_sc, ms_ev[grp - 1])
                def mmtr_ms(E, grp=grp):
                    for j in range(4):
                        i = grp * 4 + j
                        ins = E.transpose(psD[0:8, j * 128:(j + 1) * 128],
                                          mstm[:, i * 8:(i + 1) * 8], ident[:, :])
                    return ins
                P.do("tensor", lambda E, f=mmtr_ms: f(E).then_inc(s_pe.h, 1))
                pv = s_pe.inc()
                P.wait("scalar", s_pe, pv)
                P.do("scalar", lambda E, grp=grp: E.copy(
                    ms_all[:, grp * 512:(grp + 1) * 512],
                    psD[0:8, 0:512]).then_inc(s_sc.h, 1))
                ms_ev[grp] = s_sc.inc()
            msbc_half_prev[0] = ms_ev[3]
            msbc_half_prev[1] = ms_ev[3]
            tap_dump("tMS", ms_all[:, :], s_sc, ms_ev[3])

        def rt_msbc():
            emit_msbc(0, [0, 1, 2, 3])

        rt_hooks = {1: rt_lgT, 2: lambda: rt_exp(0, 4), 3: lambda: rt_exp(4, 8),
                    4: lambda: rt_exp(8, 12),
                    5: lambda: (rt_exp(12, 16), rt_fin()),
                    6: rt_mstrans, 7: rt_msbc}

        es_rt.close()
        es_ex = es.enter_context(contextlib.ExitStack())
        sbE = mksb(es_ex)
        of32 = sbE("of32", [128, NK * TW], F32)
        out_tm = sbE("out_tm", [128, 2 * 512], F32)
        ot_free = [None, None]
        s_dst = lambda k, t: hfc[:, k * S + t * TW: k * S + (t + 1) * TW]
        of_dst = lambda k, tt: of32[:, k * TW:(k + 1) * TW]

        def emit_tail(t):
            for k in range(NK):
                P.do("vector", lambda E, k=k, t=t: E.tensor_tensor(
                    s_dst(k, t), Q_dst(k, t), A_src(k, t), ALU.add))
            P.do("vector", lambda E: E.engine_nop().then_inc(s_ve.h, 1))
            v_sb = s_ve.inc()
            P.wait("scalar", s_ve, v_sb)
            P.wait("tensor", s_ve, v_sb)
            P.wait("tensor", s_sc, s_sc.n)
            v_ln3 = emit_ln(s_dst, of_dst, gain=False, tiles=[t])
            P.wait("tensor", s_ve, v_ln3)
            for gj in range(4):
                g = t * 4 + gj
                for half in range(2):
                    sl = pool.acquire()
                    def t_tr(E, sl=sl, gj=gj, half=half):
                        for j in range(4):
                            k = half * 4 + j
                            ins = E.transpose(sl["ap"][:, j * 128:(j + 1) * 128],
                                              of32[:, k * TW + gj * 128: k * TW + (gj + 1) * 128],
                                              ident[:, :])
                        return ins
                    P.do("tensor", lambda E, f=t_tr: f(E).then_inc(s_pe.h, 1))
                    pv = s_pe.inc()
                    P.wait("scalar", s_pe, pv)
                    if ot_free[half] is not None:
                        P.wait("scalar", ot_free[half][0], ot_free[half][1])
                    P.do("scalar", lambda E, sl=sl, half=half: E.copy(
                        out_tm[:, half * 512:(half + 1) * 512],
                        sl["ap"]).then_inc(s_sc.h, 1))
                    sv = s_sc.inc()
                    sl["free"] = (s_sc, sv)
                    P.wait("sync", s_sc, sv)
                    stv = s_o[half].inc(16)
                    P.do("sync", lambda E, g=g, half=half, h=s_o[half].h: E.dma_start(
                        dY[g * 128:(g + 1) * 128, half * 512:(half + 1) * 512],
                        out_tm[:, half * 512:(half + 1) * 512]).then_inc(h, 16))
                    ot_free[half] = (s_o[half], stv)
            P.wait("vector", s_pe, s_pe.n)

        for e in range(NE):
            par_e = e % 2
            for fc in range(2):
                rt_hooks_here = rt_hooks if (e == 0 and fc == 0) else None
                def w1dma(gi, fc=fc, e=e):
                    slot = gi % 3
                    if w1_free[slot] is not None:
                        P.wait("gpsimd", s_pe, w1_free[slot])
                    src = dW1[e].rearrange("(k p) f -> p k f", p=128)[
                        :, :, fc * FCW + gi * 256: fc * FCW + (gi + 1) * 256]
                    dst = w1g[:, slot * NK * 256:(slot + 1) * NK * 256].rearrange(
                        "p (k j) -> p k j", k=NK)
                    v = s_w1[slot].inc(16)
                    P.do("gpsimd", lambda E, d=dst, s=src, h=s_w1[slot].h:
                         E.dma_start(d, s).then_inc(h, 16))
                    return v

                def w2dma(gi, fc=fc, e=e):
                    slot = gi % 2
                    if w2_free[slot] is not None:
                        P.wait("gpsimd", s_pe, w2_free[slot])
                    src = dW2[e].rearrange("(k p) f -> p k f", p=128)[
                        :, fc * 16:(fc + 1) * 16, gi * 256:(gi + 1) * 256]
                    dst = w2g[:, slot * 16 * 256:(slot + 1) * 16 * 256].rearrange(
                        "p (k j) -> p k j", k=16)
                    v = s_w2[slot].inc(16)
                    P.do("gpsimd", lambda E, d=dst, s=src, h=s_w2[slot].h:
                         E.dma_start(d, s).then_inc(h, 16))
                    return v

                w1_v = {gi: w1dma(gi) for gi in range(3)}
                w2_v = {}

                P.wait("scalar", s_sc, bprep_done[e])
                for gi in range(8):
                    slot = gi % 3
                    P.wait("tensor", s_w1[slot], w1_v[gi])
                    for mf2 in range(2):
                        mf = gi * 2 + mf2
                        gmf = fc * 16 + mf
                        for t in range(NT):
                            sl = pool.acquire()
                            def mmp1(E, sl=sl, slot=slot, mf2=mf2, t=t):
                                for k in range(NK):
                                    ins = E.matmul(
                                        sl["ap"],
                                        w1g[:, slot * NK * 256 + k * 256 + mf2 * 128:
                                            slot * NK * 256 + k * 256 + (mf2 + 1) * 128],
                                        Qb[:, k * S + t * TW: k * S + (t + 1) * TW],
                                        start=(k == 0), stop=(k == NK - 1))
                                return ins
                            P.do("tensor", lambda E, f=mmp1: f(E).then_inc(s_pe.h, 1))
                            pv = s_pe.inc()
                            P.wait("scalar", s_pe, pv)
                            P.do("scalar", lambda E, sl=sl, mf=mf, gmf=gmf, t=t,
                                 par_e=par_e: E.activation(
                                hfc[:, mf * S + t * TW: mf * S + (t + 1) * TW], sl["ap"],
                                AF.Gelu, bias=b1c[:, par_e * 32 + gmf: par_e * 32 + gmf + 1],
                                scale=1.0).then_inc(s_sc.h, 1))
                            sv = s_sc.inc()
                            sl["free"] = (s_sc, sv)
                    w1_free[slot] = s_pe.n
                    if gi + 3 < 8:
                        w1_v[gi + 3] = w1dma(gi + 3)
                    elif gi == 5:
                        w2_v[0] = w2dma(0)
                        w2_v[1] = w2dma(1)
                    if rt_hooks_here and gi in rt_hooks_here:
                        rt_hooks_here[gi]()
                v_h = s_sc.n
                if e + 1 < NE:
                    if fc == 0:
                        emit_msbc(e + 1, [0, 1])
                    else:
                        emit_msbc(e + 1, [2, 3])
                        bprep_prev = emit_bprep(e + 1)

                P.wait("vector", s_sc, max(msbc_done[e], bprep_done[e]))

                def p2unit(gi, slot, m2, t):
                    m = gi * 2 + m2
                    sl = pool.acquire()
                    def mmp2(E, sl=sl, slot=slot, m2=m2, t=t):
                        for kc in range(16):
                            ins = E.matmul(
                                sl["ap"],
                                w2g[:, slot * 16 * 256 + kc * 256 + m2 * 128:
                                    slot * 16 * 256 + kc * 256 + (m2 + 1) * 128],
                                hfc[:, kc * S + t * TW: kc * S + (t + 1) * TW],
                                start=(kc == 0), stop=(kc == 15))
                        return ins
                    P.do("tensor", lambda E, f=mmp2: f(E).then_inc(s_pe.h, 1))
                    pv = s_pe.inc()
                    P.wait("vector", s_pe, pv)
                    bcol = (b2c[:, par_e * 8 + m: par_e * 8 + m + 1]
                            if fc == 0 else zero_c[:, :])
                    if e == 0 and fc == 0:
                        P.do("vector", lambda E, sl=sl, m=m, t=t, bcol=bcol,
                             par_e=par_e: E.scalar_tensor_tensor(
                                 Pb[:, m * S + t * TW: m * S + (t + 1) * TW],
                                 sl["ap"], bcol,
                                 ms_bc[:, par_e * S + t * TW: par_e * S + (t + 1) * TW],
                                 ALU.add, ALU.mult).then_inc(s_ve.h, 1))
                    else:
                        P.do("vector", lambda E, sl=sl, t=t, bcol=bcol,
                             par_e=par_e: E.scalar_tensor_tensor(
                                 tmp_bf[:, 0:TW], sl["ap"], bcol,
                                 ms_bc[:, par_e * S + t * TW: par_e * S + (t + 1) * TW],
                                 ALU.add, ALU.mult))
                        P.do("vector", lambda E, m=m, t=t: E.tensor_tensor(
                            Pb[:, m * S + t * TW: m * S + (t + 1) * TW],
                            Pb[:, m * S + t * TW: m * S + (t + 1) * TW],
                            tmp_bf[:, 0:TW],
                            ALU.add).then_inc(s_ve.h, 1))
                    y_ev = s_ve.inc()
                    sl["free"] = (s_ve, y_ev)

                for gi in range(4):
                    slot = gi % 2
                    P.wait("tensor", s_w2[slot], w2_v[gi])
                    P.wait("tensor", s_sc, v_h)
                    if e == NE - 1 and fc == 1 and gi == 3:
                        for t in range(NT):
                            for m2 in range(2):
                                p2unit(gi, slot, m2, t)
                            emit_tail(t)
                    else:
                        for m2 in range(2):
                            for t in range(NT):
                                p2unit(gi, slot, m2, t)
                    w2_free[slot] = s_pe.n
                    if gi + 2 < 4:
                        w2_v[gi + 2] = w2dma(gi + 2)
            v_y_end[e] = s_ve.n
        v_experts = s_ve.n
        tap_dump("tYA", Pb[:, :], s_ve, v_experts)

        P.wait("sync", s_o[0], s_o[0].n)
        P.wait("sync", s_o[1], s_o[1].n)

        with nc.Block() as blk:
            blk.sync(lambda E: [f(E) for f in P.q["sync"]])
            blk.gpsimd(lambda E: [f(E) for f in P.q["gpsimd"]])
            blk.tensor(lambda E: [f(E) for f in P.q["tensor"]])
            blk.scalar(lambda E: [f(E) for f in P.q["scalar"]])
            blk.vector(lambda E: [f(E) for f in P.q["vector"]])

    return nc


def kernel(**inputs):
    xs = np.ascontiguousarray(inputs["L0_attn_x"], dtype=np.float32)
    base = {
        "ln_g": np.ascontiguousarray(inputs["L0_attn_ln_g"], np.float32).reshape(1, D),
        "ln_b": np.ascontiguousarray(inputs["L0_attn_ln_b"], np.float32).reshape(1, D),
        "mix_w": np.ascontiguousarray(inputs["L0_attn_mix_w"], np.float32).reshape(D, 3),
        "mix_b": np.ascontiguousarray(inputs["L0_attn_mix_b"], np.float32).reshape(1, D),
        "Wi": np.ascontiguousarray(inputs["L0_attn_Wi"], np.float32),
        "bi": np.ascontiguousarray(inputs["L0_attn_bi"], np.float32).reshape(1, D),
        "Wsin": np.ascontiguousarray(inputs["L0_attn_Wsin"], np.float32),
        "bsin": np.ascontiguousarray(inputs["L0_attn_bsin"], np.float32).reshape(1, SD),
        "Wd": np.ascontiguousarray(inputs["L0_attn_Wd"], np.float32),
        "bd": np.ascontiguousarray(inputs["L0_attn_bd"], np.float32).reshape(1, SD),
        "Wsel": np.ascontiguousarray(inputs["L0_attn_Wsel"], np.float32),
        "bsel": np.ascontiguousarray(inputs["L0_attn_bsel"], np.float32).reshape(1, SD),
        "Wso": np.ascontiguousarray(inputs["L0_attn_Wso"], np.float32),
        "bso": np.ascontiguousarray(inputs["L0_attn_bso"], np.float32).reshape(1, D),
        "Wg": np.ascontiguousarray(inputs["L0_attn_Wg"], np.float32),
        "bg": np.ascontiguousarray(inputs["L0_attn_bg"], np.float32).reshape(1, D),
        "Wout": np.ascontiguousarray(inputs["L0_attn_Wout"], np.float32),
        "bout": np.ascontiguousarray(inputs["L0_attn_bout"], np.float32).reshape(1, D),
        "w_red": np.ascontiguousarray(inputs["L0_ffn_w_red"], np.float32),
        "wg8": np.ascontiguousarray(inputs["L0_ffn_wg"], np.float32),
        "w1": np.ascontiguousarray(inputs["L0_ffn_w1"], np.float32),
        "b1": np.ascontiguousarray(inputs["L0_ffn_b1"], np.float32),
        "w2": np.ascontiguousarray(inputs["L0_ffn_w2"], np.float32),
        "b2": np.ascontiguousarray(inputs["L0_ffn_b2"], np.float32),
        "ident": np.eye(128, dtype=np.float32),
        "identb": np.eye(8, dtype=np.float32).astype(ml_dtypes.bfloat16),
        "e8": np.repeat(np.eye(8, dtype=np.float32), 128, axis=1).astype(ml_dtypes.bfloat16),
        "ones_f": np.ones((1, 128), np.float32),
        "ones_b": np.ones((1, TW), ml_dtypes.bfloat16),
        "ones_c": np.ones((128, 1), ml_dtypes.bfloat16),
        "zero_c": np.zeros((128, 1), np.float32),
    }
    nc = build(taps=False)
    in_maps = [dict(base, x=np.ascontiguousarray(xs[b])) for b in range(8)]
    res = run_bass_kernel_spmd(nc, in_maps, core_ids=list(range(8)))
    return np.stack([np.asarray(r["y"], np.float32) for r in res.results], axis=0)
